# version 12 (final)
# baseline (speedup 1.0000x reference)
import numpy as np
import ml_dtypes
import concourse.bass as bass
import concourse.mybir as mybir
from concourse.bass_utils import run_bass_kernel_spmd
from contextlib import ExitStack

F32 = mybir.dt.float32
BF16 = mybir.dt.bfloat16
AF = mybir.ActivationFunctionType
ALU = mybir.AluOpType
AX = mybir.AxisListType

DIM = 4096
INTER = 8192
G = 8
CS = 128
KCONV = 4
NPROJ = 18560
EPS = 1e-5

T_OWN = 1024
T_PRE = 3072
T_ALL = 4096
N_PREBLK = 3
NCH = 8
NKT = 32
N_OT = 145
OT_X0, OT_B0, OT_C0, OT_DT = 64, 128, 136, 144
MASKVAL = -1e30

DEBUG = []
PHASES = "P1243F5"
DBG_SCRATCH = False
A3LVL = 9


class Arena:
    def __init__(self, nc, ctx, cols):
        self.t = ctx.enter_context(nc.sbuf_tensor("arena", [128, cols], F32))
        self.cols = cols
        self.top = 0

    def alloc(self, cols_f32, name=""):
        cols_f32 = (cols_f32 + 7) & ~7
        assert self.top + cols_f32 <= self.cols, \
            f"arena overflow at {name}: {self.top}+{cols_f32}>{self.cols}"
        off = self.top
        self.top += cols_f32
        return off

    def f32(self, off, cols, parts=128, p0=0):
        return self.t[p0:p0 + parts, off:off + cols]

    def bf16(self, off, cols, parts=128, p0=0):
        full = self.t[:].bitcast(BF16)
        return full[p0:p0 + parts, 2 * off:2 * off + cols]


class Prog:
    ENGINES = ("sync", "scalar", "vector", "tensor", "gpsimd")

    def __init__(self, nc, ctx):
        self.nc = nc
        self.tape = []
        self.sems = {}
        self.counts = {}
        self.ctx = ctx

    def sem(self, name):
        if name not in self.sems:
            self.sems[name] = self.ctx.enter_context(self.nc.semaphore(name))
            self.counts[name] = 0
        return self.sems[name]

    def cnt(self, name):
        self.sem(name)
        return self.counts[name]

    def op(self, engine, fn, inc=(), wait=()):
        for s, _ in inc:
            self.sem(s)
        for s, _ in wait:
            self.sem(s)
        self.tape.append((engine, fn, list(inc), list(wait)))
        for s, a in inc:
            self.counts[s] += a

    def dma(self, engine, out, in_, sem="ld"):
        self.op(engine, lambda e, o=out, i=in_: e.dma_start(out=o, in_=i),
                inc=[(sem, 16)])

    def trans(self, out, in_):
        self.op("scalar", lambda e, o=out, i=in_: e.dma_start_transpose(o, i),
                inc=[("tr", 16)])

    def wait_now(self, engine, *sems):
        w = [(s, self.counts[s]) for s in sems
             if self.counts.get(s, 0) > 0]
        if w:
            self.op(engine, None, wait=w)

    def barrier(self, *sems):
        for e in self.ENGINES:
            self.wait_now(e, *sems)

    def full_barrier(self):
        live = [s for s, c in self.counts.items() if c > 0]
        self.barrier(*live)

    CB_SEMS = ("ev", "ve", "tr", "sp", "rl", "ld", "gp")
    CB_ENGS = ("scalar", "vector", "sync")

    def cb(self):
        w = [(s, self.counts[s]) for s in self.CB_SEMS if self.counts.get(s, 0) > 0]
        if not w:
            return
        for e in self.CB_ENGS:
            self.op(e, None, wait=w)

    def replay(self, block):
        def mk(name):
            def body(eng):
                waited = {}
                for engine, fn, inc, wait in self.tape:
                    if engine != name:
                        continue
                    for s, v in wait:
                        if waited.get(s, -1) < v:
                            eng.wait_ge(self.sems[s], v)
                            waited[s] = v
                    inst = fn(eng) if fn is not None else None
                    for s, a in inc:
                        inst.then_inc(self.sems[s], a)
            return body
        for name in self.ENGINES:
            getattr(block, name)(mk(name))


def build_program():
    nc = bass.Bass("TRN2", target_bir_lowering=False)

    d_xp = nc.dram_tensor("xp", [T_ALL, DIM], F32, kind="ExternalInput").ap()
    d_padmask = nc.dram_tensor("padmask", [128, 32], F32, kind="ExternalInput").ap()
    d_win = nc.dram_tensor("win", [N_OT, 128, DIM], F32, kind="ExternalInput").ap()
    d_wout = nc.dram_tensor("wout", [32, 128, INTER], F32, kind="ExternalInput").ap()
    d_wup = nc.dram_tensor("wup", [32, 128, DIM], F32, kind="ExternalInput").ap()
    d_gatew = nc.dram_tensor("gatew", [128, 128], F32, kind="ExternalInput").ap()
    d_ebias = nc.dram_tensor("ebias", [1, 4], F32, kind="ExternalInput").ap()
    d_loraA = nc.dram_tensor("loraA", [32, 128, 256], F32, kind="ExternalInput").ap()
    d_loraB = nc.dram_tensor("loraB", [4, 64, DIM], F32, kind="ExternalInput").ap()
    d_convw = nc.dram_tensor("convw", [128, 80 * KCONV], F32, kind="ExternalInput").ap()
    d_convb = nc.dram_tensor("convb", [128, 80], F32, kind="ExternalInput").ap()
    d_dtb = nc.dram_tensor("dtb", [128, 1], F32, kind="ExternalInput").ap()
    d_alog = nc.dram_tensor("alog", [128, 1], F32, kind="ExternalInput").ap()
    d_drow = nc.dram_tensor("drow", [1, 128], F32, kind="ExternalInput").ap()
    d_normw = nc.dram_tensor("normw", [128, 64], F32, kind="ExternalInput").ap()
    d_ones = nc.dram_tensor("onesrow", [1, 128], F32, kind="ExternalInput").ap()
    d_ident = nc.dram_tensor("ident", [128, 128], F32, kind="ExternalInput").ap()
    d_maskneg = nc.dram_tensor("maskneg", [128, 128], F32, kind="ExternalInput").ap()
    d_fconst = nc.dram_tensor("fconst", [128, 4], F32, kind="ExternalInput").ap()

    d_out = nc.dram_tensor("out", [T_OWN, DIM], F32, kind="ExternalOutput").ap()

    KSCR = "ExternalOutput" if DBG_SCRATCH else "Internal"
    d_xt = nc.dram_tensor("sp_xt", [NCH, 128, INTER], BF16, kind=KSCR).ap()
    d_gate = nc.dram_tensor("sp_gate", [NCH, 128, INTER], BF16, kind=KSCR).ap()
    d_z = nc.dram_tensor("sp_z", [NCH, 128, INTER], BF16, kind=KSCR).ap()
    d_part = nc.dram_tensor("sp_part", [NCH, 128, DIM], F32, kind=KSCR).ap()
    d_base = nc.dram_tensor("sp_base", [NCH, 128, DIM], BF16, kind=KSCR).ap()

    dbg = {}
    for name, shape, dt in DEBUG:
        dbg[name] = nc.dram_tensor("dbg_" + name, list(shape), dt,
                                   kind="ExternalOutput").ap()

    ctx = ExitStack()
    with ctx:
        ar = Arena(nc, ctx, 51200)
        pr = Prog(nc, ctx)

        pT = ctx.enter_context(nc.psum_tensor("pT", [128, 1024], F32))
        pY1 = ctx.enter_context(nc.psum_tensor("pY1", [128, 1024], F32))
        pY2 = ctx.enter_context(nc.psum_tensor("pY2", [128, 1024], F32))
        pM = ctx.enter_context(nc.psum_tensor("pM", [128, 512], F32))
        pX = ctx.enter_context(nc.psum_tensor("pX", [128, 512], F32))

        o_ones = ar.alloc(128, "ones")
        o_ident = ar.alloc(128, "ident")
        o_identb = ar.alloc(64, "identb")
        o_maskneg = ar.alloc(128, "maskneg")
        o_dplane = ar.alloc(128, "dplane")
        o_convw = ar.alloc(320, "convw")
        o_convb = ar.alloc(80, "convb")
        o_dtb = ar.alloc(8, "dtb")
        o_negA = ar.alloc(8, "negA")
        o_padm = ar.alloc(32, "padm")
        o_gatew = ar.alloc(64, "gatew")
        o_normw = ar.alloc(64, "normw")
        o_ebc = ar.alloc(8, "ebiasbc")
        o_drow = ar.alloc(128, "drow")
        o_zero = ar.alloc(128, "zero")
        o_halo = ar.alloc(128, "halo")
        o_sq = ar.alloc(16, "sumsq")
        o_sm = ar.alloc(128, "smalls")
        o_fc = ar.alloc(8, "fconst")
        o_R = ar.alloc(8192, "R")

        ones_f = ar.f32(o_ones, 128, parts=1)
        ident_f = ar.f32(o_ident, 128)
        ident_b = ar.bf16(o_identb, 128)
        maskneg_f = ar.f32(o_maskneg, 128)
        dplane_f = ar.f32(o_dplane, 128)
        dtb_c = ar.f32(o_dtb, 1)
        negA_c = ar.f32(o_negA, 1)
        zero_f = ar.f32(o_zero, 128)
        halo_b = ar.bf16(o_halo, 240)
        R = ar.f32(o_R, 8192)

        o_RA = ar.alloc(16384, "REG_A")
        o_RB = ar.alloc(16384, "REG_B")
        o_RC = ar.alloc(8192, "REG_C")
        assert ar.top <= 51200, ar.top

        oB_slab = o_RB
        oB_work = o_RB + 4096
        oB_dt = o_RB + 14336
        oB_cumA = o_RB + 15360

        def slab_b(i):
            return ar.bf16(oB_slab + (i % 2) * 2048, 4096)

        pr.dma("sync", ones_f, d_ones)
        pr.dma("sync", ident_f, d_ident)
        pr.dma("sync", maskneg_f, d_maskneg)
        pr.dma("sync", ar.f32(o_convw, 320), d_convw)
        pr.dma("sync", ar.f32(o_convb, 80), d_convb)
        pr.dma("sync", dtb_c, d_dtb)
        pr.dma("sync", ar.f32(o_sm, 1), d_alog)
        pr.dma("sync", ar.f32(o_padm, 32), d_padmask)
        pr.dma("sync", ar.f32(o_normw, 64), d_normw)
        pr.dma("sync", ar.f32(o_sm + 8, 4, parts=1), d_ebias)
        pr.dma("sync", ar.f32(o_drow, 128, parts=1), d_drow)
        pr.dma("sync", ar.f32(o_fc, 4), d_fconst)
        pr.dma("gpsimd", ident_b, d_ident, sem="gp")
        pr.dma("gpsimd", ar.bf16(o_gatew, 128), d_gatew, sem="gp")
        pr.full_barrier()
        pr.op("scalar", lambda e: e.activation(ar.f32(o_sm + 16, 1), ar.f32(o_sm, 1), AF.Exp),
              inc=[("ev", 1)])
        pr.full_barrier()
        pr.op("vector", lambda e: e.tensor_scalar(negA_c, ar.f32(o_sm + 16, 1), -1.0, None,
                                                  ALU.mult), inc=[("ve", 1)])
        pr.op("vector", lambda e: e.memset(zero_f, 0.0), inc=[("ve", 1)])
        pr.op("vector", lambda e: e.memset(halo_b, 0.0), inc=[("ve", 1)])
        pr.op("vector", lambda e: e.memset(R, 0.0), inc=[("ve", 1)])
        pr.full_barrier()
        pr.op("tensor", lambda e: e.matmul(pM[:, 0:128], ones_f, ar.f32(o_drow, 128, parts=1),
                                           start=True, stop=True), inc=[("mm", 1)])
        pr.op("tensor", lambda e: e.matmul(pM[:, 128:132], ones_f, ar.f32(o_sm + 8, 4, parts=1),
                                           start=True, stop=True), inc=[("mm", 1)])
        pr.full_barrier()
        pr.op("vector", lambda e: e.tensor_copy(dplane_f, pM[:, 0:128]), inc=[("ve", 1)])
        pr.op("vector", lambda e: e.tensor_copy(ar.f32(o_ebc, 4), pM[:, 128:132]), inc=[("ve", 1)])
        pr.full_barrier()

        def ln_tile(src_rows, xnf_off, tcol, scratch):
            o_xt, o_sqs = scratch, scratch + 4096
            o_xnt = scratch + 4096
            xt = ar.f32(o_xt, DIM)
            xnt = ar.bf16(o_xnt, DIM)
            s1 = ar.f32(o_sm + 24, 1)
            s2 = ar.f32(o_sm + 32, 1)
            mean = ar.f32(o_sm + 40, 1)
            var = ar.f32(o_sm + 48, 1)
            rstd = ar.f32(o_sm + 56, 1)
            mrs = ar.f32(o_sm + 64, 1)
            xnf = ar.bf16(xnf_off, 2 * 16384)
            pr.dma("sync", xt, src_rows)
            pr.full_barrier()
            pr.op("scalar", lambda e: e.activation(ar.f32(o_sqs, 1024), xt[:, 0:1024],
                                                   AF.Square, accum_out=s2), inc=[("ev", 1)])
            for j in range(1, 4):
                pr.op("scalar", lambda e, j=j: e.activation(
                    ar.f32(o_sqs, 1024), xt[:, j * 1024:(j + 1) * 1024], AF.Square,
                    accum_out=ar.f32(o_sm + 72 + 8 * (j - 1), 1)), inc=[("ev", 1)])
            pr.op("vector", lambda e: e.reduce_sum(s1, xt, AX.X), inc=[("ve", 1)])
            pr.full_barrier()
            for j in range(1, 4):
                pr.op("vector", lambda e, j=j: e.tensor_add(
                    s2, s2, ar.f32(o_sm + 72 + 8 * (j - 1), 1)), inc=[("ve", 1)])
                pr.full_barrier()
            pr.op("vector", lambda e: e.tensor_scalar(mean, s1, 1.0 / DIM, None, ALU.mult),
                  inc=[("ve", 1)])
            pr.full_barrier()
            pr.op("vector", lambda e: e.scalar_tensor_tensor(var, mean, -1.0, mean,
                                                             ALU.mult, ALU.mult), inc=[("ve", 1)])
            pr.full_barrier()
            pr.op("vector", lambda e: e.scalar_tensor_tensor(var, s2, 1.0 / DIM, var,
                                                             ALU.mult, ALU.add), inc=[("ve", 1)])
            pr.full_barrier()
            pr.op("scalar", lambda e: e.activation(rstd, var, AF.Ln, bias=ar.f32(o_fc, 1)), inc=[("ev", 1)])
            pr.full_barrier()
            pr.op("scalar", lambda e: e.activation(rstd, rstd, AF.Exp, scale=-0.5), inc=[("ev", 1)])
            pr.full_barrier()
            pr.op("vector", lambda e: e.tensor_scalar(mrs, mean, rstd, None, ALU.mult), inc=[("ve", 1)])
            pr.full_barrier()
            pr.op("vector", lambda e: e.tensor_scalar(mrs, mrs, -1.0, None, ALU.mult), inc=[("ve", 1)])
            pr.full_barrier()
            pr.op("vector", lambda e: e.tensor_scalar(xnt, xt, rstd, mrs, ALU.mult, ALU.add),
                  inc=[("ve", 1)])
            pr.full_barrier()
            for kt in range(NKT):
                pr.trans(xnf[:, kt * 1024 + tcol: kt * 1024 + tcol + 128],
                         xnt[:, kt * 128:(kt + 1) * 128])
            pr.full_barrier()

        def softplus(dst, src_ap, n, scr):
            t1 = ar.f32(scr, n)
            t2 = ar.f32(scr + n, n)
            pr.op("scalar", lambda e: e.activation(t1, src_ap, AF.Abs, bias=dtb_c), inc=[("ev", 1)])
            pr.op("scalar", lambda e: e.activation(t1, t1, AF.Exp, scale=-1.0), inc=[("ev", 1)])
            pr.op("scalar", lambda e: e.activation(t1, t1, AF.Ln, bias=1.0), inc=[("ev", 1)])
            pr.op("scalar", lambda e: e.activation(t2, src_ap, AF.Relu, bias=dtb_c), inc=[("ev", 1)])
            pr.cb()
            pr.op("vector", lambda e: e.tensor_add(dst, t1, t2), inc=[("ve", 1)])
            pr.cb()

        def dt_consume(o_dtf, o_cumA, scr, psum=None, ipval=None):
            psum = psum if psum is not None else pT
            wip = [("ip", ipval)] if ipval is not None else []
            raw = ar.f32(scr + 2048, 1024)
            pr.op("scalar", lambda e, psum=psum: e.copy(raw[:, 0:512], psum[:, 0:512]),
                  inc=[("ev", 1)], wait=wip)
            pr.op("scalar", lambda e, psum=psum: e.copy(raw[:, 512:1024], psum[:, 512:1024]),
                  inc=[("ev", 1)])
            if ipval is not None:
                pr.op("scalar", lambda e: e.copy(ar.f32(o_sm + 104, 1), ar.f32(o_sm + 96, 1)),
                      inc=[("ipc", 1)])
            pr.cb()
            softplus(ar.f32(o_dtf, 1024), raw, 1024, scr)
            pr.op("vector", lambda e: e.tensor_scalar(raw, ar.f32(o_dtf, 1024), negA_c,
                                                      None, ALU.mult), inc=[("ve", 1)])
            for c in range(NCH):
                pr.op("vector", lambda e, c=c: e.tensor_tensor_scan(
                    ar.f32(o_cumA, 1024)[:, c * 128:(c + 1) * 128],
                    raw[:, c * 128:(c + 1) * 128], zero_f, 0.0, ALU.add, ALU.add),
                    inc=[("ve", 1)])
            pr.cb()

        def w1_and_dc(o_cumA, o_dtf, o_w1tm, o_dcrow, scr):
            e1 = ar.f32(scr, 128)
            w1f = ar.f32(scr + 128, 128)
            for c in range(NCH):
                cum = ar.f32(o_cumA, 1024)[:, c * 128:(c + 1) * 128]
                last = ar.f32(o_cumA, 1024)[:, c * 128 + 127:c * 128 + 128]
                pr.cb()
                pr.op("scalar", lambda e, cum=cum, last=last: e.activation(
                    e1, cum, AF.Exp, scale=-1.0, bias=last), inc=[("ev", 1)])
                pr.cb()
                pr.op("vector", lambda e, c=c: e.tensor_mul(
                    w1f, ar.f32(o_dtf, 1024)[:, c * 128:(c + 1) * 128], e1), inc=[("ve", 1)])
                pr.op("tensor", lambda e: e.transpose(pX[:, 0:128], w1f, ident_f),
                      wait=[("ve", pr.cnt("ve")), ("ev", pr.cnt("ev"))], inc=[("mm", 1)])
                pr.op("vector", lambda e, c=c: e.tensor_copy(
                    ar.f32(o_w1tm, 1024)[:, c * 128:(c + 1) * 128], pX[:, 0:128]),
                    wait=[("mm", pr.cnt("mm"))], inc=[("ve", 1)])
                pr.op("tensor", lambda e, last=last: e.transpose(
                    pX[0:1, 128:256], last, ident_f),
                    wait=[("ve", pr.cnt("ve"))], inc=[("mm", 1)])
                pr.op("scalar", lambda e, c=c: e.activation(
                    ar.f32(o_dcrow, 1024, parts=1)[:, c * 128:(c + 1) * 128],
                    pX[0:1, 128:256], AF.Exp),
                    wait=[("mm", pr.cnt("mm"))], inc=[("ev", 1)])
            pr.cb()

        def conv_tile(t80, o_pre, o_cout, psum=None, ipval=None):
            psum = psum if psum is not None else pT
            pre = ar.bf16(o_pre, 1040)
            out = ar.bf16(o_cout, 1024)
            acc = ar.f32(o_cout + 512, 1024)
            wip = [("ip", ipval)] if ipval is not None else []
            pr.op("scalar", lambda e, pre=pre, psum=psum: e.copy(pre[:, 3:515], psum[:, 0:512]),
                  inc=[("ev", 1)], wait=wip)
            pr.op("scalar", lambda e, pre=pre, psum=psum: e.copy(pre[:, 515:1027], psum[:, 512:1024]),
                  inc=[("ev", 1)])
            if ipval is not None:
                pr.op("scalar", lambda e: e.copy(ar.f32(o_sm + 104, 1), ar.f32(o_sm + 96, 1)),
                      inc=[("ipc", 1)])
            pr.op("vector", lambda e, pre=pre, t80=t80: e.tensor_copy(
                pre[:, 0:3], halo_b[:, t80 * 3:t80 * 3 + 3]), inc=[("ve", 1)])
            pr.cb()
            pr.op("vector", lambda e, pre=pre, t80=t80: e.tensor_copy(
                halo_b[:, t80 * 3:t80 * 3 + 3], pre[:, 1024:1027]), inc=[("ve", 1)])
            pr.op("vector", lambda e, acc=acc, pre=pre, t80=t80: e.tensor_scalar(
                acc, pre[:, 0:1024], ar.f32(o_convw + t80 * 4, 1), None, ALU.mult),
                inc=[("ve", 1)])
            for k in range(1, KCONV):
                pr.op("vector", lambda e, acc=acc, pre=pre, t80=t80, k=k: e.scalar_tensor_tensor(
                    acc, pre[:, k:k + 1024], ar.f32(o_convw + t80 * 4 + k, 1),
                    acc, ALU.mult, ALU.add), inc=[("ve", 1)])
            pr.cb()
            pr.op("scalar", lambda e, out=out, acc=acc, t80=t80: e.activation(
                out, acc, AF.Silu, bias=ar.f32(o_convb + t80, 1)),
                inc=[("ev", 1)], wait=[("tr", pr.cnt("tr"))])
            pr.cb()

        def stream_mm_tiles(srcs, consume, nk=NKT, slab_fn=None, prep=None):
            xnf = ar.bf16(o_RA, 2 * 16384)
            sfn = slab_fn or slab_b
            if pr.cnt("ipc") > 0:
                pr.op("tensor", None, wait=[("ipc", pr.cnt("ipc"))])
            if pr.cnt("ip") > 0:
                pr.op("gpsimd", None, wait=[("ip", pr.cnt("ip"))])
            mm_marks, c_marks = [], []
            for i, (key, src) in enumerate(srcs):
                w = [("ip", mm_marks[i - 2])] if i >= 2 else []
                pr.op("gpsimd", lambda e, i=i, src=src, sfn=sfn: e.dma_start(
                    out=sfn(i), in_=src), inc=[("wl", 16)], wait=w)
                if prep is not None:
                    prep(i, sfn(i))
                psum = pT if i % 2 == 0 else pY2
                wmm = [("wl", pr.cnt("wl"))]
                if prep is not None:
                    wmm.append(("ve", pr.cnt("ve")))
                if i >= 2:
                    wmm.append(("ipc", c_marks[i - 2]))
                pr.op("tensor", None, wait=wmm)
                for h in range(2):
                    for k in range(nk):
                        last = (h == 1 and k == nk - 1)
                        pr.op("tensor", lambda e, i=i, k=k, h=h, psum=psum, sfn=sfn, nk=nk:
                              e.matmul(
                            psum[:, h * 512:(h + 1) * 512],
                            sfn(i)[:, k * 128:(k + 1) * 128],
                            xnf[:, k * 1024 + h * 512:k * 1024 + (h + 1) * 512],
                            start=(k == 0), stop=(k == nk - 1)),
                            inc=[("ip", 1)] if last else ())
                mm_marks.append(pr.cnt("ip"))
                consume(key, psum, mm_marks[-1])
                pr.cb()
                c_marks.append(pr.cnt("ipc"))

        def inproj_tiles(ot_list, consume):
            stream_mm_tiles([(ot, d_win[ot, :, :]) for ot in ot_list],
                            lambda ot, psum, ipval: consume(ot, psum, ipval))

        if "P" in PHASES:
            o_scr6 = oB_work
            o_pre = oB_work + 6200
            o_cout = oB_work + 6720
            oC_w1 = o_RC
            oC_dc = o_RC + 1024
            oC_btm = o_RC + 2048
            oC_xg = o_RC + 2560

            for blk in range(N_PREBLK):
                for i in range(8):
                    ln_tile(d_xp[blk * 1024 + i * 128: blk * 1024 + (i + 1) * 128, :],
                            o_RA, i * 128, o_scr6)

                def dt_cons(ot, psum, ipval, blk=blk):
                    dt_consume(oB_dt, oB_cumA, o_scr6, psum, ipval)
                    w1_and_dc(oB_cumA, oB_dt, oC_w1, oC_dc, o_scr6 + 3072)
                inproj_tiles([OT_DT], dt_cons)

                for g in range(G):
                    def x_cons(ot, psum, ipval, g=g):
                        j = ot - (OT_X0 + 8 * g)
                        conv_tile(ot - OT_X0, o_pre, o_cout, psum, ipval)
                        for c in range(NCH):
                            pr.trans(ar.bf16(oC_xg, 8192)[:, c * 1024 + j * 128:
                                                          c * 1024 + (j + 1) * 128],
                                     ar.bf16(o_cout, 1024)[:, c * 128:(c + 1) * 128])
                    inproj_tiles(list(range(OT_X0 + 8 * g, OT_X0 + 8 * g + 8)), x_cons)

                    def b_cons(ot, psum, ipval, g=g):
                        conv_tile(ot - OT_X0, o_pre, o_cout, psum, ipval)
                        for c in range(NCH):
                            pr.trans(ar.bf16(oC_btm, 1024)[:, c * 128:(c + 1) * 128],
                                     ar.bf16(o_cout, 1024)[:, c * 128:(c + 1) * 128])
                    inproj_tiles([OT_B0 + g], b_cons)

                    pr.cb()
                    for c in range(NCH):
                        tile32 = blk * 8 + c
                        xg3 = ar.bf16(oC_xg, 8192)[:, c * 1024:(c + 1) * 1024].rearrange(
                            "p (h o) -> p h o", h=16)
                        w1x = ar.f32(oC_w1, 1024)[:, c * 128 + 16 * g: c * 128 + 16 * (g + 1)] \
                            .rearrange("p (h o) -> p h o", o=1).broadcast_to([128, 16, 64])
                        pr.op("vector", lambda e, xg3=xg3, w1x=w1x, tile32=tile32:
                              e.scalar_tensor_tensor(xg3, xg3,
                                                     ar.f32(o_padm + tile32, 1), w1x,
                                                     ALU.mult, ALU.mult), inc=[("ve", 1)])

                    for c in range(NCH):
                        btm = ar.bf16(oC_btm, 1024)[:, c * 128:(c + 1) * 128]
                        xg = ar.bf16(oC_xg, 8192)
                        pr.op("tensor", None, wait=[("ve", pr.cnt("ve"))])
                        pr.op("tensor", lambda e, btm=btm, xg=xg, c=c: e.matmul(
                            pY1[:, 0:512], btm, xg[:, c * 1024:c * 1024 + 512],
                            start=True, stop=True), inc=())
                        pr.op("tensor", lambda e, btm=btm, xg=xg, c=c: e.matmul(
                            pY1[:, 512:1024], btm, xg[:, c * 1024 + 512:(c + 1) * 1024],
                            start=True, stop=True), inc=[("mm", 1)])
                        dcr = ar.f32(oC_dc, 1024, parts=1)[:, c * 128:(c + 1) * 128]
                        pr.op("tensor", lambda e, dcr=dcr: e.matmul(
                            pX[:, 128:256], ones_f, dcr, start=True, stop=True),
                            inc=[("mm", 1)])
                        Rg = R[:, g * 1024:(g + 1) * 1024]
                        dcx = pX[:, 128 + 16 * g:128 + 16 * (g + 1)].rearrange(
                            "p (h o) -> p h o", o=1).broadcast_to([128, 16, 64])
                        pr.op("vector", lambda e, Rg=Rg, dcx=dcx: e.tensor_tensor(
                            Rg.rearrange("p (h o) -> p h o", h=16),
                            Rg.rearrange("p (h o) -> p h o", h=16), dcx, ALU.mult),
                            wait=[("mm", pr.cnt("mm"))], inc=[("ve", 1)])
                        pr.op("vector", lambda e, Rg=Rg: e.tensor_add(Rg, Rg, pY1[:]),
                              inc=[("ve", 1)])
                    pr.op("scalar", None, wait=[("mm", pr.cnt("mm"))])

                if blk == N_PREBLK - 1:
                    def c_cons(ot, psum, ipval):
                        t80 = ot - OT_X0
                        pre = ar.bf16(o_pre, 1040)
                        pr.op("scalar", lambda e, pre=pre, psum=psum: e.copy(
                            pre[:, 3:515], psum[:, 0:512]),
                            inc=[("ev", 1)], wait=[("ip", ipval)])
                        pr.op("scalar", lambda e, pre=pre, psum=psum: e.copy(
                            pre[:, 515:1027], psum[:, 512:1024]), inc=[("ev", 1)])
                        pr.op("scalar", lambda e: e.copy(ar.f32(o_sm + 104, 1), ar.f32(o_sm + 96, 1)),
                              inc=[("ipc", 1)])
                        pr.cb()
                        pr.op("vector", lambda e, t80=t80, pre=pre: e.tensor_copy(
                            halo_b[:, t80 * 3:t80 * 3 + 3], pre[:, 1024:1027]), inc=[("ve", 1)])
                    inproj_tiles(list(range(OT_C0, OT_C0 + 8)), c_cons)

        if "R" in dbg:
            pr.dma("sync", dbg["R"], R, sem="sp")
            pr.full_barrier()

        if "1" in PHASES:
            o_scr6 = oB_work
            for i in range(8):
                ln_tile(d_xp[T_PRE + i * 128: T_PRE + (i + 1) * 128, :], o_RA,
                        i * 128, o_scr6)

        if "2" in PHASES:
            o_scr6 = oB_work
            o_pre = oB_work + 6200
            o_cout = oB_work + 6720
            o_tmg = oB_work + 1024

            def dt_cons2(ot, psum, ipval):
                dt_consume(oB_dt, oB_cumA, o_scr6, psum, ipval)
            inproj_tiles([OT_DT], dt_cons2)

            for g in range(G):
                def x_cons2(ot, psum, ipval, g=g):
                    j = ot - (OT_X0 + 8 * g)
                    conv_tile(ot - OT_X0, o_pre, o_cout, psum, ipval)
                    for c in range(NCH):
                        pr.trans(ar.bf16(o_tmg, 8192)[:, c * 1024 + j * 128:
                                                      c * 1024 + (j + 1) * 128],
                                 ar.bf16(o_cout, 1024)[:, c * 128:(c + 1) * 128])
                inproj_tiles(list(range(OT_X0 + 8 * g, OT_X0 + 8 * g + 8)), x_cons2)
                pr.op("sync", None, wait=[("tr", pr.cnt("tr")), ("ipc", pr.cnt("ipc"))])
                for c in range(NCH):
                    pr.dma("sync", d_xt[c, :, g * 1024:(g + 1) * 1024],
                           ar.bf16(o_tmg, 8192)[:, c * 1024:(c + 1) * 1024], sem="sp")
                pr.op("scalar", None, wait=[("sp", pr.cnt("sp"))])

                def b_cons2(ot, psum, ipval, g=g):
                    conv_tile(ot - OT_X0, o_pre, o_cout, psum, ipval)
                    pr.op("vector", lambda e, g=g: e.tensor_copy(
                        ar.bf16(o_RC, 8192)[:, g * 1024:(g + 1) * 1024],
                        ar.bf16(o_cout, 1024)), inc=[("ve", 1)])
                inproj_tiles([OT_B0 + g], b_cons2)

                def c_cons2(ot, psum, ipval, g=g):
                    conv_tile(ot - OT_X0, o_pre, o_cout, psum, ipval)
                    pr.op("vector", lambda e, g=g: e.tensor_copy(
                        ar.bf16(o_RC + 4096, 8192)[:, g * 1024:(g + 1) * 1024],
                        ar.bf16(o_cout, 1024)), inc=[("ve", 1)])
                inproj_tiles([OT_C0 + g], c_cons2)

            for gg in range(8):
                def gate_cons(ot, psum, ipval, gg=gg):
                    j = ot - 8 * gg
                    out = ar.bf16(o_cout, 1024)
                    pr.op("scalar", lambda e, out=out, psum=psum: e.copy(
                        out[:, 0:512], psum[:, 0:512]),
                        inc=[("ev", 1)], wait=[("ip", ipval), ("tr", pr.cnt("tr"))])
                    pr.op("scalar", lambda e, out=out, psum=psum: e.copy(
                        out[:, 512:1024], psum[:, 512:1024]), inc=[("ev", 1)])
                    pr.op("scalar", lambda e: e.copy(ar.f32(o_sm + 104, 1), ar.f32(o_sm + 96, 1)),
                          inc=[("ipc", 1)])
                    for c in range(NCH):
                        pr.trans(ar.bf16(o_tmg, 8192)[:, c * 1024 + j * 128:
                                                      c * 1024 + (j + 1) * 128],
                                 out[:, c * 128:(c + 1) * 128])
                inproj_tiles(list(range(8 * gg, 8 * gg + 8)), gate_cons)
                pr.op("sync", None, wait=[("tr", pr.cnt("tr")), ("ipc", pr.cnt("ipc"))])
                for c in range(NCH):
                    pr.dma("sync", d_gate[c, :, gg * 1024:(gg + 1) * 1024],
                           ar.bf16(o_tmg, 8192)[:, c * 1024:(c + 1) * 1024], sem="sp")
                pr.op("scalar", None, wait=[("sp", pr.cnt("sp"))])

        if "xnf" in dbg:
            pr.dma("sync", dbg["xnf"], ar.bf16(o_RA, 32768), sem="sp")
            pr.full_barrier()
        if "bf" in dbg:
            pr.dma("sync", dbg["bf"], ar.bf16(o_RC, 8192), sem="sp")
            pr.full_barrier()
        if "cf" in dbg:
            pr.dma("sync", dbg["cf"], ar.bf16(o_RC + 4096, 8192), sem="sp")
            pr.full_barrier()
        if "dtf" in dbg:
            pr.dma("sync", dbg["dtf"], ar.f32(oB_dt, 1024), sem="sp")
            pr.full_barrier()
        if "cumA" in dbg:
            pr.dma("sync", dbg["cumA"], ar.f32(oB_cumA, 1024), sem="sp")
            pr.full_barrier()

        if "4" in PHASES:
            oW = oB_work
            o_msk = oW
            o_zl = oW + 64
            o_apad = oW + 1088
            o_bpad = oW + 1344
            o_bev = oW + 3392
            o_btm4 = oW + 3904
            o_xs = oW + 5952
            o_acc = oW + 6976
            o_pcs = oW + 8000
            o_lg = oW + 8512
            xnf = ar.bf16(o_RA, 2 * 16384)

            for t in range(8):
                pr.wait_now("tensor", "ve", "ev", "gp")
                for k in range(NKT):
                    pr.op("tensor", lambda e, t=t, k=k: e.matmul(
                        pM[:, 0:4], xnf[:, k * 1024 + t * 128:k * 1024 + (t + 1) * 128],
                        ar.bf16(o_gatew, 128)[:, 4 * k:4 * k + 4],
                        start=(k == 0), stop=(k == NKT - 1)),
                        inc=[("mm", 1)] if k == NKT - 1 else ())
                pr.full_barrier()
                lg = ar.f32(o_lg, 8)
                pr.op("vector", lambda e, lg=lg: e.tensor_add(lg[:, 0:4], pM[:, 0:4],
                                                              ar.f32(o_ebc, 4)), inc=[("ve", 1)])
                pr.full_barrier()
                mx = ar.f32(o_sm + 24, 1)
                nd = ar.f32(o_sm + 32, 1)
                tmp = ar.f32(o_sm + 40, 1)
                pr.op("vector", lambda e, lg=lg, mx=mx: e.reduce_max(mx, lg[:, 0:4], AX.X),
                      inc=[("ve", 1)])
                pr.full_barrier()
                for e_ in range(4):
                    mcol = ar.f32(o_msk, 32)[:, t * 4 + e_:t * 4 + e_ + 1]
                    pr.op("vector", lambda e, lg=lg, mx=mx, mcol=mcol, e_=e_: e.tensor_tensor(
                        mcol, lg[:, e_:e_ + 1], mx, ALU.is_ge), inc=[("ve", 1)])
                pr.full_barrier()
                pr.op("vector", lambda e, nd=nd: e.memset(nd, 1.0), inc=[("ve", 1)])
                pr.full_barrier()
                for e_ in range(4):
                    mcol = ar.f32(o_msk, 32)[:, t * 4 + e_:t * 4 + e_ + 1]
                    if e_ > 0:
                        pr.op("vector", lambda e, mcol=mcol, nd=nd: e.tensor_mul(mcol, mcol, nd),
                              inc=[("ve", 1)])
                        pr.full_barrier()
                    if e_ < 3:
                        pr.op("vector", lambda e, mcol=mcol, tmp=tmp, nd=nd: e.tensor_mul(
                            tmp, nd, mcol), inc=[("ve", 1)])
                        pr.full_barrier()
                        pr.op("vector", lambda e, nd=nd, tmp=tmp: e.tensor_sub(nd, nd, tmp),
                              inc=[("ve", 1)])
                        pr.full_barrier()

            for half in range(2):
                for k in range(NKT):
                    apk = ar.bf16(o_apad + (k % 2) * 128, 256)
                    w = [("ip", pr.cnt("ip") - 1)] if k >= 2 else [("ve", pr.cnt("ve"))]
                    pr.op("gpsimd", lambda e, k=k, apk=apk: e.dma_start(
                        out=apk, in_=d_loraA[k, :, :]), inc=[("wl", 16)], wait=w)
                    pr.op("tensor", None, wait=[("wl", pr.cnt("wl"))])
                    for hh in range(2):
                        pr.op("tensor", lambda e, k=k, half=half, hh=hh, apk=apk: e.matmul(
                            pY1[:, hh * 512:(hh + 1) * 512],
                            apk[:, half * 128:(half + 1) * 128],
                            xnf[:, k * 1024 + hh * 512:k * 1024 + (hh + 1) * 512],
                            start=(k == 0), stop=(k == NKT - 1)),
                            inc=[("ip", 1)] if hh == 1 else ())
                pr.op("vector", lambda e, half=half: e.tensor_copy(
                    ar.bf16(o_zl + half * 512, 1024), pY1[:]),
                    wait=[("ip", pr.cnt("ip"))], inc=[("ve", 1)])
            pr.full_barrier()

            def base_cons(ot, psum, ipval):
                bev = ar.bf16(o_bev, 1024)
                pr.op("vector", lambda e, bev=bev, psum=psum: e.tensor_copy(
                    bev[:, 0:512], psum[:, 0:512]), inc=[("ve", 1)],
                    wait=[("ip", ipval), ("tr", pr.cnt("tr"))])
                pr.op("vector", lambda e, bev=bev, psum=psum: e.tensor_copy(
                    bev[:, 512:1024], psum[:, 512:1024]), inc=[("ve", 1)])
                pr.op("vector", lambda e: e.tensor_copy(ar.f32(o_sm + 104, 1),
                                                        ar.f32(o_sm + 96, 1)), inc=[("ipc", 1)])
                pr.op("scalar", None, wait=[("ve", pr.cnt("ve")), ("sp", pr.cnt("sp"))])
                for t in range(8):
                    pr.trans(ar.bf16(o_pcs, 1024)[:, t * 128:(t + 1) * 128],
                             bev[:, t * 128:(t + 1) * 128])
                pr.op("sync", None, wait=[("tr", pr.cnt("tr"))])
                for t in range(8):
                    pr.dma("sync", d_base[t, :, ot * 128:(ot + 1) * 128],
                           ar.bf16(o_pcs, 1024)[:, t * 128:(t + 1) * 128], sem="sp")
            stream_mm_tiles([(ot, d_wup[ot, :, :]) for ot in range(32)], base_cons)
            pr.full_barrier()

            pr.full_barrier()
            ebank = [pY2[:, 0:512], pY2[:, 512:1024], pT[:, 0:512], pT[:, 512:1024]]
            sp_marks, mm_marks4, ve_marks4 = [], [], []
            it = 0
            for t in range(8):
                pr.op("sync", lambda e, t=t: e.dma_start(
                    out=ar.bf16(o_btm4, 4096), in_=d_base[t, :, :]),
                    inc=[("rl", 16)], wait=[("ve", pr.cnt("ve"))])
                for os_ in range(8):
                    xs = ar.f32(o_xs + (it % 2) * 512, 512)
                    acc = ar.f32(o_acc + (it % 2) * 512, 512)
                    pr.op("sync", lambda e, xs=xs, t=t, os_=os_: e.dma_start(
                        out=xs, in_=d_xp[T_PRE + t * 128:T_PRE + (t + 1) * 128,
                                         os_ * 512:(os_ + 1) * 512]), inc=[("rl", 16)])
                    for e_ in range(4):
                        p0 = (e_ % 2) * 64
                        bpad = ar.bf16(o_bpad + (it % 2) * 1024 + e_ * 256, 512,
                                       parts=64, p0=p0)
                        w = [("ip", mm_marks4[it - 2])] if it >= 2 else []
                        pr.op("gpsimd", lambda e, e_=e_, os_=os_, bpad=bpad: e.dma_start(
                            out=bpad,
                            in_=d_loraB[e_, :, os_ * 512:(os_ + 1) * 512]),
                            inc=[("wl", 16)], wait=w if e_ == 0 else [])
                    wv = [("rl", pr.cnt("rl"))]
                    if it >= 2:
                        wv.append(("sp", sp_marks[it - 2]))
                    pr.op("vector", lambda e, acc=acc, xs=xs, os_=os_: e.tensor_add(
                        acc, ar.bf16(o_btm4, 4096)[:, os_ * 512:(os_ + 1) * 512], xs),
                        inc=[("ve", 1)], wait=wv)
                    wmm = [("wl", pr.cnt("wl"))]
                    if it >= 1:
                        wmm.append(("ve", ve_marks4[it - 1]))
                    pr.op("tensor", None, wait=wmm)
                    for e_ in range(4):
                        p0 = (e_ % 2) * 64
                        bpad = ar.bf16(o_bpad + (it % 2) * 1024 + e_ * 256, 512,
                                       parts=64, p0=p0)
                        zsl = ar.bf16(o_zl, 2048)[p0:p0 + 64,
                                                  (e_ // 2) * 1024 + t * 128:
                                                  (e_ // 2) * 1024 + (t + 1) * 128]
                        pr.op("tensor", lambda e, zsl=zsl, bpad=bpad, e_=e_: e.matmul(
                            ebank[e_], zsl, bpad, start=True, stop=True),
                            inc=[("ip", 1)] if e_ == 3 else ())
                    mm_marks4.append(pr.cnt("ip"))
                    pr.op("vector", None, wait=[("ip", mm_marks4[-1])])
                    for e_ in range(4):
                        mcol = ar.f32(o_msk, 32)[:, t * 4 + e_:t * 4 + e_ + 1]
                        pr.op("vector", lambda e, acc=acc, mcol=mcol, e_=e_:
                              e.scalar_tensor_tensor(acc, ebank[e_], mcol, acc,
                                                     ALU.mult, ALU.add), inc=[("ve", 1)])
                    ve_marks4.append(pr.cnt("ve"))
                    pr.op("sync", None, wait=[("ve", pr.cnt("ve"))])
                    pr.dma("sync", d_part[t, :, os_ * 512:(os_ + 1) * 512], acc, sem="sp")
                    sp_marks.append(pr.cnt("sp"))
                    it += 1
            pr.full_barrier()

        if "3" in PHASES:
            o_dttm = o_RA
            o_w1tm = o_RA + 1024
            o_ecatm = o_RA + 2048
            o_ncatm = o_RA + 3072
            o_xtm = o_RA + 4096
            o_y = o_RA + 8192
            o_ytmp = o_RA + 12288
            o_Tb = o_RA + 13312
            o_Mb = o_RA + 13824
            o_sg3 = o_RA + 14336
            o_rbf = o_RA + 15872
            o_gt = oB_work
            o_zc = oB_work + 4096
            o_flat = oB_work + 8192
            o_btm3 = oB_slab
            o_dc3 = oB_slab + 64
            o_w1scr = oB_slab + 1088

            dtf = ar.f32(oB_dt, 1024)
            cumA = ar.f32(oB_cumA, 1024)

            for c in range(NCH if A3LVL >= -1 else 0):
                pr.op("tensor", lambda e, c=c: e.transpose(
                    pX[:, 0:128], cumA[:, c * 128:(c + 1) * 128], ident_f), inc=[("mm", 1)])
                pr.full_barrier()
                pr.op("vector", lambda e, c=c: e.tensor_scalar(
                    ar.f32(o_ncatm, 1024)[:, c * 128:(c + 1) * 128], pX[:, 0:128],
                    -1.0, None, ALU.mult), inc=[("ve", 1)])
                pr.full_barrier()
                pr.op("scalar", lambda e, c=c: e.activation(
                    ar.f32(o_ecatm, 1024)[:, c * 128:(c + 1) * 128],
                    ar.f32(o_ncatm, 1024)[:, c * 128:(c + 1) * 128],
                    AF.Exp, scale=-1.0), inc=[("ev", 1)])
                pr.full_barrier()
                pr.op("tensor", lambda e, c=c: e.transpose(
                    pX[:, 0:128], dtf[:, c * 128:(c + 1) * 128], ident_f), inc=[("mm", 1)])
                pr.full_barrier()
                pr.op("vector", lambda e, c=c: e.tensor_copy(
                    ar.f32(o_dttm, 1024)[:, c * 128:(c + 1) * 128], pX[:, 0:128]),
                    inc=[("ve", 1)])
                pr.full_barrier()
            if A3LVL >= -1:
                w1_and_dc(oB_cumA, oB_dt, o_w1tm, o_dc3, o_w1scr)

            for c in range(NCH if A3LVL >= 0 else 0):
                pr.dma("sync", ar.bf16(o_xtm, 8192), d_xt[c, :, :], sem="rl")
                pr.dma("sync", ar.bf16(o_gt, 8192), d_gate[c, :, :], sem="rl")
                pr.full_barrier()
                if A3LVL >= 1:
                    pr.op("scalar", lambda e: e.activation(
                        ar.bf16(o_gt, 8192), ar.bf16(o_gt, 8192), AF.Silu), inc=[("ev", 1)])
                    pr.full_barrier()
                for g in range(G if A3LVL >= 2 else 0):
                    pr.op("gpsimd", lambda e, c=c, g=g: e.dma_start(
                        out=ar.f32(o_flat, 2048, parts=1),
                        in_=cumA[16 * g:16 * (g + 1), c * 128:(c + 1) * 128]),
                        inc=[("gp", 16)], wait=[("tr", pr.cnt("tr"))])
                    pr.op("scalar", None, wait=[("gp", pr.cnt("gp"))])
                    pr.trans(ar.bf16(o_btm3, 128),
                             ar.bf16(o_RC, 8192)[:, g * 1024 + c * 128:
                                                 g * 1024 + (c + 1) * 128])
                    Rg = R[:, g * 1024:(g + 1) * 1024]
                    pr.op("vector", lambda e, Rg=Rg: e.tensor_copy(
                        ar.bf16(o_rbf, 1024), Rg), inc=[("ve", 1)])
                    xsl3 = ar.bf16(o_xtm, 8192)[:, g * 1024:(g + 1) * 1024].rearrange(
                        "p (h o) -> p h o", h=16)
                    dt_ap = ar.f32(o_dttm, 1024)[:, c * 128 + 16 * g: c * 128 + 16 * (g + 1)] \
                        .rearrange("p (h o) -> p h o", o=1).broadcast_to([128, 16, 64])
                    w1_ap = ar.f32(o_w1tm, 1024)[:, c * 128 + 16 * g: c * 128 + 16 * (g + 1)] \
                        .rearrange("p (h o) -> p h o", o=1).broadcast_to([128, 16, 64])
                    eca_ap = ar.f32(o_ecatm, 1024)[:, c * 128 + 16 * g: c * 128 + 16 * (g + 1)] \
                        .rearrange("p (h o) -> p h o", o=1).broadcast_to([128, 16, 64])
                    dpl_ap = dplane_f[:, 16 * g:16 * (g + 1)].rearrange(
                        "p (h o) -> p h o", o=1).broadcast_to([128, 16, 64])
                    xdt3 = ar.bf16(o_sg3, 1024).rearrange("p (h o) -> p h o", h=16)
                    xt3 = ar.bf16(o_sg3 + 512, 1024).rearrange("p (h o) -> p h o", h=16)
                    xD3 = ar.bf16(o_sg3 + 1024, 1024).rearrange("p (h o) -> p h o", h=16)
                    pmcol = ar.f32(o_padm + 24 + c, 1)
                    pr.op("vector", lambda e, xdt3=xdt3, xsl3=xsl3, dt_ap=dt_ap:
                          e.tensor_tensor(xdt3, xsl3, dt_ap, ALU.mult), inc=[("ve", 1)])
                    pr.op("vector", lambda e, xt3=xt3, xsl3=xsl3, pmcol=pmcol, w1_ap=w1_ap:
                          e.scalar_tensor_tensor(xt3, xsl3, pmcol, w1_ap, ALU.mult, ALU.mult),
                          inc=[("ve", 1)])
                    pr.op("vector", lambda e, xD3=xD3, xsl3=xsl3, dpl_ap=dpl_ap:
                          e.tensor_tensor(xD3, xsl3, dpl_ap, ALU.mult), inc=[("ve", 1)])
                    pr.full_barrier()

                    Bf = ar.bf16(o_RC, 8192)[:, g * 1024 + c * 128: g * 1024 + (c + 1) * 128]
                    Cf = ar.bf16(o_RC + 4096, 8192)[:, g * 1024 + c * 128:
                                                    g * 1024 + (c + 1) * 128]
                    if A3LVL < 3:
                        continue
                    pr.op("tensor", lambda e, Bf=Bf, Cf=Cf: e.matmul(
                        pM[:, 0:128], Bf, Cf, start=True, stop=True), inc=[("mm", 1)])
                    pr.full_barrier()

                    for half in range(2 if A3LVL >= 4 else 0):
                        for h8 in range(8):
                            hloc = half * 8 + h8
                            fsl = ar.f32(o_flat, 2048, parts=1)[:, hloc * 128:(hloc + 1) * 128]
                            pr.op("tensor", lambda e, fsl=fsl, h8=h8: e.matmul(
                                pT[:, h8 * 128:(h8 + 1) * 128], ones_f, fsl,
                                start=True, stop=False), inc=())
                            pr.op("tensor", lambda e, h8=h8: e.matmul(
                                pT[:, h8 * 128:(h8 + 1) * 128], ident_f, maskneg_f,
                                start=False, stop=True),
                                inc=[("mm", 1)] if h8 == 7 else ())
                        pr.full_barrier()
                        for h8 in range(8):
                            hloc = half * 8 + h8
                            bcol = ar.f32(o_ncatm, 1024)[:, c * 128 + 16 * g + hloc:
                                                         c * 128 + 16 * g + hloc + 1]
                            pr.op("scalar", lambda e, h8=h8, bcol=bcol: e.activation(
                                ar.bf16(o_Tb, 1024)[:, h8 * 128:(h8 + 1) * 128],
                                pT[:, h8 * 128:(h8 + 1) * 128], AF.Exp, bias=bcol),
                                inc=[("ev", 1)])
                        pr.full_barrier()
                        gmx = pM[:, 0:128].rearrange("p (o l) -> p o l", o=1) \
                            .broadcast_to([128, 8, 128])
                        pr.op("vector", lambda e, gmx=gmx: e.tensor_tensor(
                            ar.bf16(o_Mb, 1024).rearrange("p (h l) -> p h l", h=8),
                            ar.bf16(o_Tb, 1024).rearrange("p (h l) -> p h l", h=8),
                            gmx, ALU.mult), inc=[("ve", 1)])
                        pr.full_barrier()
                        for h8 in range(8):
                            hloc = half * 8 + h8
                            pr.op("tensor", lambda e, h8=h8, hloc=hloc: e.matmul(
                                pY1[:, hloc * 64:(hloc + 1) * 64],
                                ar.bf16(o_Mb, 1024)[:, h8 * 128:(h8 + 1) * 128],
                                ar.bf16(o_sg3, 1024)[:, hloc * 64:(hloc + 1) * 64],
                                start=True, stop=False), inc=())
                            pr.op("tensor", lambda e, hloc=hloc: e.matmul(
                                pY1[:, hloc * 64:(hloc + 1) * 64], ident_b,
                                ar.bf16(o_sg3 + 1024, 1024)[:, hloc * 64:(hloc + 1) * 64],
                                start=False, stop=True),
                                inc=[("mm", 1)] if hloc == 15 else ())
                    if A3LVL < 5:
                        continue
                    for hh in range(2):
                        pr.op("tensor", lambda e, hh=hh, Cf=Cf: e.matmul(
                            pY2[:, hh * 512:(hh + 1) * 512], Cf,
                            ar.bf16(o_rbf, 1024)[:, hh * 512:(hh + 1) * 512],
                            start=True, stop=True), inc=[("mm", 1)] if hh else ())
                    pr.full_barrier()
                    pY23 = pY2[:].rearrange("p (h o) -> p h o", h=16)
                    yt3 = ar.f32(o_ytmp, 1024).rearrange("p (h o) -> p h o", h=16)
                    pr.op("vector", lambda e, yt3=yt3, pY23=pY23, eca_ap=eca_ap:
                          e.tensor_tensor(yt3, pY23, eca_ap, ALU.mult), inc=[("ve", 1)])
                    pr.full_barrier()
                    pr.op("vector", lambda e, g=g: e.tensor_add(
                        ar.bf16(o_y, 8192)[:, g * 1024:(g + 1) * 1024],
                        ar.f32(o_ytmp, 1024), pY1[:]), inc=[("ve", 1)])
                    pr.full_barrier()
                    pr.op("scalar", lambda e, g=g: e.activation(
                        ar.f32(o_ytmp, 1024), ar.bf16(o_y, 8192)[:, g * 1024:(g + 1) * 1024],
                        AF.Square, accum_out=ar.f32(o_sq + g, 1)), inc=[("ev", 1)])
                    pr.full_barrier()
                    if A3LVL < 6:
                        continue
                    for hh in range(2):
                        pr.op("tensor", lambda e, hh=hh: e.matmul(
                            pY1[:, hh * 512:(hh + 1) * 512], ar.bf16(o_btm3, 128),
                            ar.bf16(o_sg3 + 512, 1024)[:, hh * 512:(hh + 1) * 512],
                            start=True, stop=True), inc=[("mm", 1)] if hh else ())
                    dcr = ar.f32(o_dc3, 1024, parts=1)[:, c * 128:(c + 1) * 128]
                    pr.op("tensor", lambda e, dcr=dcr: e.matmul(
                        pX[:, 128:256], ones_f, dcr, start=True, stop=True), inc=[("mm", 1)])
                    pr.full_barrier()
                    dcx = pX[:, 128 + 16 * g:128 + 16 * (g + 1)].rearrange(
                        "p (h o) -> p h o", o=1).broadcast_to([128, 16, 64])
                    pr.op("vector", lambda e, Rg=Rg, dcx=dcx: e.tensor_tensor(
                        Rg.rearrange("p (h o) -> p h o", h=16),
                        Rg.rearrange("p (h o) -> p h o", h=16), dcx, ALU.mult),
                        inc=[("ve", 1)])
                    pr.full_barrier()
                    pr.op("vector", lambda e, Rg=Rg: e.tensor_add(Rg, Rg, pY1[:]),
                          inc=[("ve", 1)])
                    pr.full_barrier()

                if A3LVL == 0:
                    continue
                pr.op("vector", lambda e: e.reduce_sum(ar.f32(o_sq + 8, 1),
                                                       ar.f32(o_sq, 8), AX.X), inc=[("ve", 1)])
                pr.full_barrier()
                pr.op("scalar", lambda e: e.activation(ar.f32(o_sq + 8, 1), ar.f32(o_sq + 8, 1),
                                                       AF.Ln, bias=ar.f32(o_fc, 1), scale=1.0 / INTER),
                      inc=[("ev", 1)])
                pr.full_barrier()
                pr.op("scalar", lambda e: e.activation(ar.f32(o_sq + 8, 1), ar.f32(o_sq + 8, 1),
                                                       AF.Exp, scale=-0.5), inc=[("ev", 1)])
                pr.full_barrier()
                pr.op("vector", lambda e: e.scalar_tensor_tensor(
                    ar.bf16(o_zc, 8192), ar.bf16(o_y, 8192), ar.f32(o_sq + 8, 1),
                    ar.bf16(o_gt, 8192), ALU.mult, ALU.mult), inc=[("ve", 1)])
                pr.full_barrier()
                pr.dma("sync", d_z[c, :, :], ar.bf16(o_zc, 8192), sem="sp")
                pr.full_barrier()

        if "5" in PHASES:
            def zf_tile(kt):
                if kt < 32:
                    return ar.bf16(o_RA + kt * 512, 1024)
                if kt < 48:
                    return ar.bf16(o_RC + (kt - 32) * 512, 1024)
                return ar.bf16(o_R + (kt - 48) * 512, 1024)

            o_ztm = o_RB
            o_aev = o_RB + 4096
            o_pc5 = o_RB + 4608
            o_pp5 = o_RB + 5120
            o_ob5 = o_RB + 6144
            oW5 = o_RB + 7168

            def wslab5(i):
                return ar.bf16(oW5 + (i % 2) * 4096, 8192)

            for c in range(NCH):
                pr.op("sync", lambda e, c=c: e.dma_start(
                    out=ar.bf16(o_ztm, 8192), in_=d_z[c, :, :]),
                    inc=[("rl", 16)], wait=[("tr", pr.cnt("tr"))])
                pr.op("scalar", None, wait=[("rl", pr.cnt("rl"))])
                for kt in range(64):
                    pr.trans(zf_tile(kt)[:, c * 128:(c + 1) * 128],
                             ar.bf16(o_ztm, 8192)[:, kt * 128:(kt + 1) * 128])
            pr.full_barrier()

            nwx = ar.f32(o_normw, 64).rearrange("p (k o) -> p k o", o=1) \
                .broadcast_to([128, 64, 128])
            mm_marks5, c_marks5 = [], []
            for ot in range(32):
                w = [("ip", mm_marks5[ot - 2])] if ot >= 2 else []
                pr.op("gpsimd", lambda e, ot=ot: e.dma_start(
                    out=wslab5(ot), in_=d_wout[ot, :, :]), inc=[("wl", 16)], wait=w)
                pr.op("vector", lambda e, ot=ot, nwx=nwx: e.tensor_tensor(
                    wslab5(ot).rearrange("p (k o) -> p k o", k=64),
                    wslab5(ot).rearrange("p (k o) -> p k o", k=64), nwx, ALU.mult),
                    inc=[("ve", 1)], wait=[("wl", pr.cnt("wl"))])
                psum = pT if ot % 2 == 0 else pY1
                wmm = [("ve", pr.cnt("ve"))]
                if ot >= 2:
                    wmm.append(("ipc", c_marks5[ot - 2]))
                pr.op("tensor", None, wait=wmm)
                for kt in range(64):
                    for hh in range(2):
                        pr.op("tensor", lambda e, ot=ot, kt=kt, hh=hh, psum=psum: e.matmul(
                            psum[:, hh * 512:(hh + 1) * 512],
                            wslab5(ot)[:, kt * 128:(kt + 1) * 128],
                            zf_tile(kt)[:, hh * 512:(hh + 1) * 512],
                            start=(kt == 0), stop=(kt == 63)),
                            inc=[("ip", 1)] if (kt == 63 and hh == 1) else ())
                mm_marks5.append(pr.cnt("ip"))
                aev = ar.bf16(o_aev, 1024)
                pr.op("vector", lambda e, aev=aev, psum=psum: e.tensor_copy(
                    aev[:, 0:512], psum[:, 0:512]),
                    inc=[("ve", 1)], wait=[("ip", mm_marks5[-1]), ("fin", pr.cnt("fin"))])
                pr.op("vector", lambda e, aev=aev, psum=psum: e.tensor_copy(
                    aev[:, 512:1024], psum[:, 512:1024]), inc=[("ve", 1)])
                pr.op("scalar", None, wait=[("ve", pr.cnt("ve"))])
                for t in range(8):
                    pr.trans(ar.bf16(o_pc5, 1024)[:, t * 128:(t + 1) * 128],
                             aev[:, t * 128:(t + 1) * 128])
                pr.op("sync", None, wait=[("ve", pr.cnt("ve"))])
                for t in range(8):
                    pr.dma("sync", ar.f32(o_pp5, 1024)[:, t * 128:(t + 1) * 128],
                           d_part[t, :, ot * 128:(ot + 1) * 128], sem="rl")
                pr.op("vector", lambda e: e.tensor_add(
                    ar.f32(o_ob5, 1024), ar.bf16(o_pc5, 1024), ar.f32(o_pp5, 1024)),
                    inc=[("ve", 1)],
                    wait=[("tr", pr.cnt("tr")), ("rl", pr.cnt("rl"))])
                pr.op("vector", lambda e: e.tensor_copy(ar.f32(o_sm + 96, 1),
                                                        ar.f32(o_sm + 96, 1)),
                      inc=[("ipc", 1)])
                c_marks5.append(pr.cnt("ipc"))
                pr.op("sync", None, wait=[("ve", pr.cnt("ve"))])
                for t in range(8):
                    pr.dma("sync", d_out[t * 128:(t + 1) * 128, ot * 128:(ot + 1) * 128],
                           ar.f32(o_ob5, 1024)[:, t * 128:(t + 1) * 128], sem="fin")
            pr.full_barrier()

        pr.barrier("fin")

        block = ctx.enter_context(nc.Block())
        pr.replay(block)
    return nc


def _host_inputs(inputs):
    f = np.float32
    x = np.ascontiguousarray(inputs["L0_attn_hidden_states"], dtype=f)
    W_in = np.ascontiguousarray(inputs["L0_attn_in_proj_w"], dtype=f)
    conv_w = np.ascontiguousarray(inputs["L0_attn_conv_w"], dtype=f)
    conv_b = np.ascontiguousarray(inputs["L0_attn_conv_b"], dtype=f)
    dt_bias = np.ascontiguousarray(inputs["L0_attn_dt_bias"], dtype=f)
    A_log = np.ascontiguousarray(inputs["L0_attn_A_log"], dtype=f)
    D = np.ascontiguousarray(inputs["L0_attn_D"], dtype=f)
    norm_w = np.ascontiguousarray(inputs["L0_attn_norm_w"], dtype=f)
    W_out = np.ascontiguousarray(inputs["L0_attn_out_proj_w"], dtype=f)
    W_up = np.ascontiguousarray(inputs["L0_ffn_W_up"], dtype=f)
    gate_w = np.ascontiguousarray(inputs["L0_ffn_gate_w"], dtype=f)
    ebias = np.ascontiguousarray(inputs["L0_ffn_expert_bias"], dtype=f)
    lora_A = np.ascontiguousarray(inputs["L0_ffn_lora_A"], dtype=f)
    lora_B = np.ascontiguousarray(inputs["L0_ffn_lora_B"], dtype=f)

    win = np.ascontiguousarray(
        W_in.reshape(N_OT, 128, NKT, 128).transpose(0, 3, 2, 1).reshape(N_OT, 128, DIM))
    wout = np.ascontiguousarray(
        W_out.reshape(32, 128, 64, 128).transpose(0, 3, 2, 1).reshape(32, 128, INTER))
    wup = np.ascontiguousarray(
        W_up.reshape(32, 128, 32, 128).transpose(0, 3, 2, 1).reshape(32, 128, DIM))
    gatew = np.ascontiguousarray(
        gate_w.T.reshape(32, 128, 4).transpose(1, 0, 2).reshape(128, 128))
    loraA = np.zeros((DIM, 256), f)
    for e in range(4):
        loraA[:, e * 64:e * 64 + 8] = lora_A[e].T
    loraA = np.ascontiguousarray(loraA.reshape(32, 128, 256))
    loraB = np.zeros((4, 64, DIM), f)
    for e in range(4):
        loraB[e, 0:8, :] = lora_B[e].T
    convw = np.ascontiguousarray(
        conv_w[:, 0, :].reshape(80, 128, KCONV).transpose(1, 0, 2).reshape(128, 80 * KCONV))
    convb = np.ascontiguousarray(conv_b.reshape(80, 128).T)
    normw = np.ascontiguousarray(norm_w.reshape(64, 128).T)

    ones = np.ones((1, 128), f)
    ident = np.eye(128, dtype=f)
    maskneg = np.where(np.arange(128)[None, :] < np.arange(128)[:, None],
                       f(MASKVAL), f(0.0))

    shared = dict(win=win, wout=wout, wup=wup, gatew=gatew,
                  ebias=ebias.reshape(1, 4), loraA=loraA, loraB=loraB,
                  convw=convw, convb=convb, dtb=dt_bias.reshape(128, 1),
                  alog=A_log.reshape(128, 1), drow=D.reshape(1, 128),
                  normw=normw, onesrow=ones, ident=ident, maskneg=maskneg,
                  fconst=np.full((128, 4), EPS, dtype=f))

    in_maps = []
    for k in range(8):
        b, q = divmod(k, 4)
        start = q * T_OWN
        xp = np.zeros((T_ALL, DIM), f)
        xp[T_PRE - start:T_PRE] = x[b, :start]
        xp[T_PRE:] = x[b, start:start + T_OWN]
        pm = np.zeros((128, 32), f)
        pm[:, (T_PRE - start) // 128:] = 1.0
        m = dict(shared)
        m["xp"] = xp
        m["padmask"] = pm
        in_maps.append(m)
    return in_maps


def kernel(**inputs):
    in_maps = _host_inputs(inputs)
    nc = build_program()
    res = run_bass_kernel_spmd(nc, in_maps, core_ids=list(range(8)))
    out = np.empty((2, 4096, DIM), np.float32)
    for k in range(8):
        b, q = divmod(k, 4)
        out[b, q * T_OWN:(q + 1) * T_OWN] = res.results[k]["out"]
    return out


# version 14
# speedup vs baseline: 1.0925x; 1.0925x over previous
import numpy as np
import ml_dtypes
import concourse.bass as bass
import concourse.mybir as mybir
from concourse.bass_utils import run_bass_kernel_spmd
from contextlib import ExitStack

F32 = mybir.dt.float32
BF16 = mybir.dt.bfloat16
AF = mybir.ActivationFunctionType
ALU = mybir.AluOpType
AX = mybir.AxisListType

DIM = 4096
INTER = 8192
G = 8
CS = 128
KCONV = 4
NPROJ = 18560
EPS = 1e-5

T_OWN = 1024
T_PRE = 3072
T_ALL = 4096
N_PREBLK = 3
NCH = 8
NKT = 32
N_OT = 145
OT_X0, OT_B0, OT_C0, OT_DT = 64, 128, 136, 144
MASKVAL = -1e30

DEBUG = []
PHASES = "P1243F5"
DBG_SCRATCH = False
A3LVL = 9


class Arena:
    def __init__(self, nc, ctx, cols):
        self.t = ctx.enter_context(nc.sbuf_tensor("arena", [128, cols], F32))
        self.cols = cols
        self.top = 0

    def alloc(self, cols_f32, name=""):
        cols_f32 = (cols_f32 + 7) & ~7
        assert self.top + cols_f32 <= self.cols, \
            f"arena overflow at {name}: {self.top}+{cols_f32}>{self.cols}"
        off = self.top
        self.top += cols_f32
        return off

    def f32(self, off, cols, parts=128, p0=0):
        return self.t[p0:p0 + parts, off:off + cols]

    def bf16(self, off, cols, parts=128, p0=0):
        full = self.t[:].bitcast(BF16)
        return full[p0:p0 + parts, 2 * off:2 * off + cols]


class Prog:
    ENGINES = ("sync", "scalar", "vector", "tensor", "gpsimd")

    def __init__(self, nc, ctx):
        self.nc = nc
        self.tape = []
        self.sems = {}
        self.counts = {}
        self.ctx = ctx

    def sem(self, name):
        if name not in self.sems:
            self.sems[name] = self.ctx.enter_context(self.nc.semaphore(name))
            self.counts[name] = 0
        return self.sems[name]

    def cnt(self, name):
        self.sem(name)
        return self.counts[name]

    def op(self, engine, fn, inc=(), wait=()):
        for s, _ in inc:
            self.sem(s)
        for s, _ in wait:
            self.sem(s)
        self.tape.append((engine, fn, list(inc), list(wait)))
        for s, a in inc:
            self.counts[s] += a

    def dma(self, engine, out, in_, sem="ld"):
        self.op(engine, lambda e, o=out, i=in_: e.dma_start(out=o, in_=i),
                inc=[(sem, 16)])

    def trans(self, out, in_):
        self.op("scalar", lambda e, o=out, i=in_: e.dma_start_transpose(o, i),
                inc=[("tr", 16)])

    def wait_now(self, engine, *sems):
        w = [(s, self.counts[s]) for s in sems
             if self.counts.get(s, 0) > 0]
        if w:
            self.op(engine, None, wait=w)

    def barrier(self, *sems):
        for e in self.ENGINES:
            self.wait_now(e, *sems)

    def full_barrier(self):
        live = [s for s, c in self.counts.items() if c > 0]
        self.barrier(*live)

    CB_SEMS = ("ev", "ve", "tr")
    CB_ENGS = ("scalar", "vector", "sync")

    def cb(self):
        w = [(s, self.counts[s]) for s in self.CB_SEMS if self.counts.get(s, 0) > 0]
        if not w:
            return
        for e in self.CB_ENGS:
            self.op(e, None, wait=w)

    def replay(self, block):
        def mk(name):
            def body(eng):
                waited = {}
                for engine, fn, inc, wait in self.tape:
                    if engine != name:
                        continue
                    for s, v in wait:
                        if waited.get(s, -1) < v:
                            eng.wait_ge(self.sems[s], v)
                            waited[s] = v
                    inst = fn(eng) if fn is not None else None
                    for s, a in inc:
                        inst.then_inc(self.sems[s], a)
            return body
        for name in self.ENGINES:
            getattr(block, name)(mk(name))


def build_program():
    nc = bass.Bass("TRN2", target_bir_lowering=False)

    d_xp = nc.dram_tensor("xp", [T_ALL, DIM], F32, kind="ExternalInput").ap()
    d_padmask = nc.dram_tensor("padmask", [128, 32], F32, kind="ExternalInput").ap()
    d_win = nc.dram_tensor("win", [N_OT, 128, DIM], F32, kind="ExternalInput").ap()
    d_wout = nc.dram_tensor("wout", [32, 128, INTER], F32, kind="ExternalInput").ap()
    d_wup = nc.dram_tensor("wup", [32, 128, DIM], F32, kind="ExternalInput").ap()
    d_gatew = nc.dram_tensor("gatew", [128, 128], F32, kind="ExternalInput").ap()
    d_ebias = nc.dram_tensor("ebias", [1, 4], F32, kind="ExternalInput").ap()
    d_loraA = nc.dram_tensor("loraA", [32, 128, 256], F32, kind="ExternalInput").ap()
    d_loraB = nc.dram_tensor("loraB", [4, 64, DIM], F32, kind="ExternalInput").ap()
    d_convw = nc.dram_tensor("convw", [128, 80 * KCONV], F32, kind="ExternalInput").ap()
    d_convb = nc.dram_tensor("convb", [128, 80], F32, kind="ExternalInput").ap()
    d_dtb = nc.dram_tensor("dtb", [128, 1], F32, kind="ExternalInput").ap()
    d_alog = nc.dram_tensor("alog", [128, 1], F32, kind="ExternalInput").ap()
    d_drow = nc.dram_tensor("drow", [1, 128], F32, kind="ExternalInput").ap()
    d_normw = nc.dram_tensor("normw", [128, 64], F32, kind="ExternalInput").ap()
    d_ones = nc.dram_tensor("onesrow", [1, 128], F32, kind="ExternalInput").ap()
    d_ident = nc.dram_tensor("ident", [128, 128], F32, kind="ExternalInput").ap()
    d_maskneg = nc.dram_tensor("maskneg", [128, 128], F32, kind="ExternalInput").ap()
    d_fconst = nc.dram_tensor("fconst", [128, 4], F32, kind="ExternalInput").ap()

    d_out = nc.dram_tensor("out", [T_OWN, DIM], F32, kind="ExternalOutput").ap()

    KSCR = "ExternalOutput" if DBG_SCRATCH else "Internal"
    d_xt = nc.dram_tensor("sp_xt", [NCH, 128, INTER], BF16, kind=KSCR).ap()
    d_gate = nc.dram_tensor("sp_gate", [NCH, 128, INTER], BF16, kind=KSCR).ap()
    d_z = nc.dram_tensor("sp_z", [NCH, 128, INTER], BF16, kind=KSCR).ap()
    d_part = nc.dram_tensor("sp_part", [NCH, 128, DIM], F32, kind=KSCR).ap()
    d_base = nc.dram_tensor("sp_base", [NCH, 128, DIM], BF16, kind=KSCR).ap()

    dbg = {}
    for name, shape, dt in DEBUG:
        dbg[name] = nc.dram_tensor("dbg_" + name, list(shape), dt,
                                   kind="ExternalOutput").ap()

    ctx = ExitStack()
    with ctx:
        ar = Arena(nc, ctx, 51200)
        pr = Prog(nc, ctx)

        pT = ctx.enter_context(nc.psum_tensor("pT", [128, 1024], F32))
        pY1 = ctx.enter_context(nc.psum_tensor("pY1", [128, 1024], F32))
        pY2 = ctx.enter_context(nc.psum_tensor("pY2", [128, 1024], F32))
        pM = ctx.enter_context(nc.psum_tensor("pM", [128, 512], F32))
        pX = ctx.enter_context(nc.psum_tensor("pX", [128, 512], F32))

        o_ones = ar.alloc(128, "ones")
        o_ident = ar.alloc(128, "ident")
        o_identb = ar.alloc(64, "identb")
        o_maskneg = ar.alloc(128, "maskneg")
        o_dplane = ar.alloc(128, "dplane")
        o_convw = ar.alloc(320, "convw")
        o_convb = ar.alloc(80, "convb")
        o_dtb = ar.alloc(8, "dtb")
        o_negA = ar.alloc(8, "negA")
        o_padm = ar.alloc(32, "padm")
        o_gatew = ar.alloc(64, "gatew")
        o_normw = ar.alloc(64, "normw")
        o_ebc = ar.alloc(8, "ebiasbc")
        o_drow = ar.alloc(128, "drow")
        o_zero = ar.alloc(128, "zero")
        o_halo = ar.alloc(128, "halo")
        o_sq = ar.alloc(16, "sumsq")
        o_sm = ar.alloc(128, "smalls")
        o_fc = ar.alloc(8, "fconst")
        o_R = ar.alloc(8192, "R")

        ones_f = ar.f32(o_ones, 128, parts=1)
        ident_f = ar.f32(o_ident, 128)
        ident_b = ar.bf16(o_identb, 128)
        maskneg_f = ar.f32(o_maskneg, 128)
        dplane_f = ar.f32(o_dplane, 128)
        dtb_c = ar.f32(o_dtb, 1)
        negA_c = ar.f32(o_negA, 1)
        zero_f = ar.f32(o_zero, 128)
        halo_b = ar.bf16(o_halo, 240)
        R = ar.f32(o_R, 8192)

        o_RA = ar.alloc(16384, "REG_A")
        o_RB = ar.alloc(16384, "REG_B")
        o_RC = ar.alloc(8192, "REG_C")
        assert ar.top <= 51200, ar.top

        oB_slab = o_RB
        oB_work = o_RB + 4096
        oB_dt = o_RB + 14336
        oB_cumA = o_RB + 15360

        def slab_b(i):
            return ar.bf16(oB_slab + (i % 2) * 2048, 4096)

        pr.dma("sync", ones_f, d_ones)
        pr.dma("sync", ident_f, d_ident)
        pr.dma("sync", maskneg_f, d_maskneg)
        pr.dma("sync", ar.f32(o_convw, 320), d_convw)
        pr.dma("sync", ar.f32(o_convb, 80), d_convb)
        pr.dma("sync", dtb_c, d_dtb)
        pr.dma("sync", ar.f32(o_sm, 1), d_alog)
        pr.dma("sync", ar.f32(o_padm, 32), d_padmask)
        pr.dma("sync", ar.f32(o_normw, 64), d_normw)
        pr.dma("sync", ar.f32(o_sm + 8, 4, parts=1), d_ebias)
        pr.dma("sync", ar.f32(o_drow, 128, parts=1), d_drow)
        pr.dma("sync", ar.f32(o_fc, 4), d_fconst)
        pr.dma("gpsimd", ident_b, d_ident, sem="gp")
        pr.dma("gpsimd", ar.bf16(o_gatew, 128), d_gatew, sem="gp")
        pr.full_barrier()
        pr.op("scalar", lambda e: e.activation(ar.f32(o_sm + 16, 1), ar.f32(o_sm, 1), AF.Exp),
              inc=[("ev", 1)])
        pr.full_barrier()
        pr.op("vector", lambda e: e.tensor_scalar(negA_c, ar.f32(o_sm + 16, 1), -1.0, None,
                                                  ALU.mult), inc=[("ve", 1)])
        pr.op("vector", lambda e: e.memset(zero_f, 0.0), inc=[("ve", 1)])
        pr.op("vector", lambda e: e.memset(halo_b, 0.0), inc=[("ve", 1)])
        pr.op("vector", lambda e: e.memset(R, 0.0), inc=[("ve", 1)])
        pr.full_barrier()
        pr.op("tensor", lambda e: e.matmul(pM[:, 0:128], ones_f, ar.f32(o_drow, 128, parts=1),
                                           start=True, stop=True), inc=[("mm", 1)])
        pr.op("tensor", lambda e: e.matmul(pM[:, 128:132], ones_f, ar.f32(o_sm + 8, 4, parts=1),
                                           start=True, stop=True), inc=[("mm", 1)])
        pr.full_barrier()
        pr.op("vector", lambda e: e.tensor_copy(dplane_f, pM[:, 0:128]), inc=[("ve", 1)])
        pr.op("vector", lambda e: e.tensor_copy(ar.f32(o_ebc, 4), pM[:, 128:132]), inc=[("ve", 1)])
        pr.full_barrier()

        def ln_tile(src_rows, xnf_off, tcol, scratch):
            o_xt, o_sqs = scratch, scratch + 4096
            o_xnt = scratch + 4096
            xt = ar.f32(o_xt, DIM)
            xnt = ar.bf16(o_xnt, DIM)
            s1 = ar.f32(o_sm + 24, 1)
            s2 = ar.f32(o_sm + 32, 1)
            mean = ar.f32(o_sm + 40, 1)
            var = ar.f32(o_sm + 48, 1)
            rstd = ar.f32(o_sm + 56, 1)
            mrs = ar.f32(o_sm + 64, 1)
            xnf = ar.bf16(xnf_off, 2 * 16384)
            pr.dma("sync", xt, src_rows)
            pr.full_barrier()
            pr.op("scalar", lambda e: e.activation(ar.f32(o_sqs, 1024), xt[:, 0:1024],
                                                   AF.Square, accum_out=s2), inc=[("ev", 1)])
            for j in range(1, 4):
                pr.op("scalar", lambda e, j=j: e.activation(
                    ar.f32(o_sqs, 1024), xt[:, j * 1024:(j + 1) * 1024], AF.Square,
                    accum_out=ar.f32(o_sm + 72 + 8 * (j - 1), 1)), inc=[("ev", 1)])
            pr.op("vector", lambda e: e.reduce_sum(s1, xt, AX.X), inc=[("ve", 1)])
            pr.full_barrier()
            for j in range(1, 4):
                pr.op("vector", lambda e, j=j: e.tensor_add(
                    s2, s2, ar.f32(o_sm + 72 + 8 * (j - 1), 1)), inc=[("ve", 1)])
                pr.full_barrier()
            pr.op("vector", lambda e: e.tensor_scalar(mean, s1, 1.0 / DIM, None, ALU.mult),
                  inc=[("ve", 1)])
            pr.full_barrier()
            pr.op("vector", lambda e: e.scalar_tensor_tensor(var, mean, -1.0, mean,
                                                             ALU.mult, ALU.mult), inc=[("ve", 1)])
            pr.full_barrier()
            pr.op("vector", lambda e: e.scalar_tensor_tensor(var, s2, 1.0 / DIM, var,
                                                             ALU.mult, ALU.add), inc=[("ve", 1)])
            pr.full_barrier()
            pr.op("scalar", lambda e: e.activation(rstd, var, AF.Ln, bias=ar.f32(o_fc, 1)), inc=[("ev", 1)])
            pr.full_barrier()
            pr.op("scalar", lambda e: e.activation(rstd, rstd, AF.Exp, scale=-0.5), inc=[("ev", 1)])
            pr.full_barrier()
            pr.op("vector", lambda e: e.tensor_scalar(mrs, mean, rstd, None, ALU.mult), inc=[("ve", 1)])
            pr.full_barrier()
            pr.op("vector", lambda e: e.tensor_scalar(mrs, mrs, -1.0, None, ALU.mult), inc=[("ve", 1)])
            pr.full_barrier()
            pr.op("vector", lambda e: e.tensor_scalar(xnt, xt, rstd, mrs, ALU.mult, ALU.add),
                  inc=[("ve", 1)])
            pr.full_barrier()
            for kt in range(NKT):
                pr.trans(xnf[:, kt * 1024 + tcol: kt * 1024 + tcol + 128],
                         xnt[:, kt * 128:(kt + 1) * 128])
            pr.full_barrier()

        def softplus(dst, src_ap, n, scr):
            t1 = ar.f32(scr, n)
            t2 = ar.f32(scr + n, n)
            pr.op("scalar", lambda e: e.activation(t1, src_ap, AF.Abs, bias=dtb_c), inc=[("ev", 1)])
            pr.op("scalar", lambda e: e.activation(t1, t1, AF.Exp, scale=-1.0), inc=[("ev", 1)])
            pr.op("scalar", lambda e: e.activation(t1, t1, AF.Ln, bias=1.0), inc=[("ev", 1)])
            pr.op("scalar", lambda e: e.activation(t2, src_ap, AF.Relu, bias=dtb_c), inc=[("ev", 1)])
            pr.cb()
            pr.op("vector", lambda e: e.tensor_add(dst, t1, t2), inc=[("ve", 1)])
            pr.cb()

        def dt_consume(o_dtf, o_cumA, scr, psum=None, ipval=None):
            psum = psum if psum is not None else pT
            wip = [("ip", ipval)] if ipval is not None else []
            raw = ar.f32(scr + 2048, 1024)
            pr.op("scalar", lambda e, psum=psum: e.copy(raw[:, 0:512], psum[:, 0:512]),
                  inc=[("ev", 1)], wait=wip)
            pr.op("scalar", lambda e, psum=psum: e.copy(raw[:, 512:1024], psum[:, 512:1024]),
                  inc=[("ev", 1)])
            if ipval is not None:
                pr.op("scalar", lambda e: e.copy(ar.f32(o_sm + 104, 1), ar.f32(o_sm + 96, 1)),
                      inc=[("ipc", 1)])
            pr.cb()
            softplus(ar.f32(o_dtf, 1024), raw, 1024, scr)
            pr.op("vector", lambda e: e.tensor_scalar(raw, ar.f32(o_dtf, 1024), negA_c,
                                                      None, ALU.mult), inc=[("ve", 1)])
            for c in range(NCH):
                pr.op("vector", lambda e, c=c: e.tensor_tensor_scan(
                    ar.f32(o_cumA, 1024)[:, c * 128:(c + 1) * 128],
                    raw[:, c * 128:(c + 1) * 128], zero_f, 0.0, ALU.add, ALU.add),
                    inc=[("ve", 1)])
            pr.cb()

        def w1_and_dc(o_cumA, o_dtf, o_w1tm, o_dcrow, scr):
            e1 = ar.f32(scr, 128)
            w1f = ar.f32(scr + 128, 128)
            for c in range(NCH):
                cum = ar.f32(o_cumA, 1024)[:, c * 128:(c + 1) * 128]
                last = ar.f32(o_cumA, 1024)[:, c * 128 + 127:c * 128 + 128]
                pr.cb()
                pr.op("scalar", lambda e, cum=cum, last=last: e.activation(
                    e1, cum, AF.Exp, scale=-1.0, bias=last), inc=[("ev", 1)])
                pr.cb()
                pr.op("vector", lambda e, c=c: e.tensor_mul(
                    w1f, ar.f32(o_dtf, 1024)[:, c * 128:(c + 1) * 128], e1), inc=[("ve", 1)])
                pr.op("tensor", lambda e: e.transpose(pX[:, 0:128], w1f, ident_f),
                      wait=[("ve", pr.cnt("ve")), ("ev", pr.cnt("ev"))], inc=[("mm", 1)])
                pr.op("vector", lambda e, c=c: e.tensor_copy(
                    ar.f32(o_w1tm, 1024)[:, c * 128:(c + 1) * 128], pX[:, 0:128]),
                    wait=[("mm", pr.cnt("mm"))], inc=[("ve", 1)])
                pr.op("tensor", lambda e, last=last: e.transpose(
                    pX[0:1, 128:256], last, ident_f),
                    wait=[("ve", pr.cnt("ve"))], inc=[("mm", 1)])
                pr.op("scalar", lambda e, c=c: e.activation(
                    ar.f32(o_dcrow, 1024, parts=1)[:, c * 128:(c + 1) * 128],
                    pX[0:1, 128:256], AF.Exp),
                    wait=[("mm", pr.cnt("mm"))], inc=[("ev", 1)])
            pr.cb()

        def conv_tile(t80, o_pre, o_cout, psum=None, ipval=None):
            psum = psum if psum is not None else pT
            pre = ar.bf16(o_pre, 1040)
            out = ar.bf16(o_cout, 1024)
            acc = ar.f32(o_cout + 512, 1024)
            wip = [("ip", ipval)] if ipval is not None else []
            pr.op("scalar", lambda e, pre=pre, psum=psum: e.copy(pre[:, 3:515], psum[:, 0:512]),
                  inc=[("ev", 1)], wait=wip)
            pr.op("scalar", lambda e, pre=pre, psum=psum: e.copy(pre[:, 515:1027], psum[:, 512:1024]),
                  inc=[("ev", 1)])
            if ipval is not None:
                pr.op("scalar", lambda e: e.copy(ar.f32(o_sm + 104, 1), ar.f32(o_sm + 96, 1)),
                      inc=[("ipc", 1)])
            pr.op("vector", lambda e, pre=pre, t80=t80: e.tensor_copy(
                pre[:, 0:3], halo_b[:, t80 * 3:t80 * 3 + 3]), inc=[("ve", 1)])
            pr.cb()
            pr.op("vector", lambda e, pre=pre, t80=t80: e.tensor_copy(
                halo_b[:, t80 * 3:t80 * 3 + 3], pre[:, 1024:1027]), inc=[("ve", 1)])
            pr.op("vector", lambda e, acc=acc, pre=pre, t80=t80: e.tensor_scalar(
                acc, pre[:, 0:1024], ar.f32(o_convw + t80 * 4, 1), None, ALU.mult),
                inc=[("ve", 1)])
            for k in range(1, KCONV):
                pr.op("vector", lambda e, acc=acc, pre=pre, t80=t80, k=k: e.scalar_tensor_tensor(
                    acc, pre[:, k:k + 1024], ar.f32(o_convw + t80 * 4 + k, 1),
                    acc, ALU.mult, ALU.add), inc=[("ve", 1)])
            pr.cb()
            pr.op("scalar", lambda e, out=out, acc=acc, t80=t80: e.activation(
                out, acc, AF.Silu, bias=ar.f32(o_convb + t80, 1)),
                inc=[("ev", 1)], wait=[("tr", pr.cnt("tr"))])
            pr.cb()

        def stream_mm_tiles(srcs, consume, nk=NKT, slab_fn=None, prep=None):
            xnf = ar.bf16(o_RA, 2 * 16384)
            sfn = slab_fn or slab_b
            if pr.cnt("ipc") > 0:
                pr.op("tensor", None, wait=[("ipc", pr.cnt("ipc"))])
            if pr.cnt("ip") > 0:
                pr.op("gpsimd", None, wait=[("ip", pr.cnt("ip"))])
            mm_marks, c_marks = [], []
            for i, (key, src) in enumerate(srcs):
                w = [("ip", mm_marks[i - 2])] if i >= 2 else []
                pr.op("gpsimd", lambda e, i=i, src=src, sfn=sfn: e.dma_start(
                    out=sfn(i), in_=src), inc=[("wl", 16)], wait=w)
                if prep is not None:
                    prep(i, sfn(i))
                psum = pT if i % 2 == 0 else pY2
                wmm = [("wl", pr.cnt("wl"))]
                if prep is not None:
                    wmm.append(("ve", pr.cnt("ve")))
                if i >= 2:
                    wmm.append(("ipc", c_marks[i - 2]))
                pr.op("tensor", None, wait=wmm)
                for h in range(2):
                    for k in range(nk):
                        last = (h == 1 and k == nk - 1)
                        pr.op("tensor", lambda e, i=i, k=k, h=h, psum=psum, sfn=sfn, nk=nk:
                              e.matmul(
                            psum[:, h * 512:(h + 1) * 512],
                            sfn(i)[:, k * 128:(k + 1) * 128],
                            xnf[:, k * 1024 + h * 512:k * 1024 + (h + 1) * 512],
                            start=(k == 0), stop=(k == nk - 1)),
                            inc=[("ip", 1)] if last else ())
                mm_marks.append(pr.cnt("ip"))
                consume(key, psum, mm_marks[-1])
                pr.cb()
                c_marks.append(pr.cnt("ipc"))

        def inproj_tiles(ot_list, consume):
            stream_mm_tiles([(ot, d_win[ot, :, :]) for ot in ot_list],
                            lambda ot, psum, ipval: consume(ot, psum, ipval))

        if "P" in PHASES:
            o_scr6 = oB_work
            o_pre = oB_work + 6200
            o_cout = oB_work + 6720
            oC_w1 = o_RC
            oC_dc = o_RC + 1024
            oC_btm = o_RC + 2048
            oC_xg = o_RC + 2560

            for blk in range(N_PREBLK):
                for i in range(8):
                    ln_tile(d_xp[blk * 1024 + i * 128: blk * 1024 + (i + 1) * 128, :],
                            o_RA, i * 128, o_scr6)

                def dt_cons(ot, psum, ipval, blk=blk):
                    dt_consume(oB_dt, oB_cumA, o_scr6, psum, ipval)
                    w1_and_dc(oB_cumA, oB_dt, oC_w1, oC_dc, o_scr6 + 3072)
                inproj_tiles([OT_DT], dt_cons)

                for g in range(G):
                    def x_cons(ot, psum, ipval, g=g):
                        j = ot - (OT_X0 + 8 * g)
                        conv_tile(ot - OT_X0, o_pre, o_cout, psum, ipval)
                        for c in range(NCH):
                            pr.trans(ar.bf16(oC_xg, 8192)[:, c * 1024 + j * 128:
                                                          c * 1024 + (j + 1) * 128],
                                     ar.bf16(o_cout, 1024)[:, c * 128:(c + 1) * 128])
                    inproj_tiles(list(range(OT_X0 + 8 * g, OT_X0 + 8 * g + 8)), x_cons)

                    def b_cons(ot, psum, ipval, g=g):
                        conv_tile(ot - OT_X0, o_pre, o_cout, psum, ipval)
                        for c in range(NCH):
                            pr.trans(ar.bf16(oC_btm, 1024)[:, c * 128:(c + 1) * 128],
                                     ar.bf16(o_cout, 1024)[:, c * 128:(c + 1) * 128])
                    inproj_tiles([OT_B0 + g], b_cons)

                    pr.cb()
                    for c in range(NCH):
                        tile32 = blk * 8 + c
                        xg3 = ar.bf16(oC_xg, 8192)[:, c * 1024:(c + 1) * 1024].rearrange(
                            "p (h o) -> p h o", h=16)
                        w1x = ar.f32(oC_w1, 1024)[:, c * 128 + 16 * g: c * 128 + 16 * (g + 1)] \
                            .rearrange("p (h o) -> p h o", o=1).broadcast_to([128, 16, 64])
                        pr.op("vector", lambda e, xg3=xg3, w1x=w1x, tile32=tile32:
                              e.scalar_tensor_tensor(xg3, xg3,
                                                     ar.f32(o_padm + tile32, 1), w1x,
                                                     ALU.mult, ALU.mult), inc=[("ve", 1)])

                    for c in range(NCH):
                        btm = ar.bf16(oC_btm, 1024)[:, c * 128:(c + 1) * 128]
                        xg = ar.bf16(oC_xg, 8192)
                        pr.op("tensor", None, wait=[("ve", pr.cnt("ve"))])
                        pr.op("tensor", lambda e, btm=btm, xg=xg, c=c: e.matmul(
                            pY1[:, 0:512], btm, xg[:, c * 1024:c * 1024 + 512],
                            start=True, stop=True), inc=())
                        pr.op("tensor", lambda e, btm=btm, xg=xg, c=c: e.matmul(
                            pY1[:, 512:1024], btm, xg[:, c * 1024 + 512:(c + 1) * 1024],
                            start=True, stop=True), inc=[("mm", 1)])
                        dcr = ar.f32(oC_dc, 1024, parts=1)[:, c * 128:(c + 1) * 128]
                        pr.op("tensor", lambda e, dcr=dcr: e.matmul(
                            pX[:, 128:256], ones_f, dcr, start=True, stop=True),
                            inc=[("mm", 1)])
                        Rg = R[:, g * 1024:(g + 1) * 1024]
                        dcx = pX[:, 128 + 16 * g:128 + 16 * (g + 1)].rearrange(
                            "p (h o) -> p h o", o=1).broadcast_to([128, 16, 64])
                        pr.op("vector", lambda e, Rg=Rg, dcx=dcx: e.tensor_tensor(
                            Rg.rearrange("p (h o) -> p h o", h=16),
                            Rg.rearrange("p (h o) -> p h o", h=16), dcx, ALU.mult),
                            wait=[("mm", pr.cnt("mm"))], inc=[("ve", 1)])
                        pr.op("vector", lambda e, Rg=Rg: e.tensor_add(Rg, Rg, pY1[:]),
                              inc=[("ve", 1)])
                    pr.op("scalar", None, wait=[("mm", pr.cnt("mm"))])

                if blk == N_PREBLK - 1:
                    def c_cons(ot, psum, ipval):
                        t80 = ot - OT_X0
                        pre = ar.bf16(o_pre, 1040)
                        pr.op("scalar", lambda e, pre=pre, psum=psum: e.copy(
                            pre[:, 3:515], psum[:, 0:512]),
                            inc=[("ev", 1)], wait=[("ip", ipval)])
                        pr.op("scalar", lambda e, pre=pre, psum=psum: e.copy(
                            pre[:, 515:1027], psum[:, 512:1024]), inc=[("ev", 1)])
                        pr.op("scalar", lambda e: e.copy(ar.f32(o_sm + 104, 1), ar.f32(o_sm + 96, 1)),
                              inc=[("ipc", 1)])
                        pr.cb()
                        pr.op("vector", lambda e, t80=t80, pre=pre: e.tensor_copy(
                            halo_b[:, t80 * 3:t80 * 3 + 3], pre[:, 1024:1027]), inc=[("ve", 1)])
                    inproj_tiles(list(range(OT_C0, OT_C0 + 8)), c_cons)

        if "R" in dbg:
            pr.dma("sync", dbg["R"], R, sem="sp")
            pr.full_barrier()

        if "1" in PHASES:
            o_scr6 = oB_work
            for i in range(8):
                ln_tile(d_xp[T_PRE + i * 128: T_PRE + (i + 1) * 128, :], o_RA,
                        i * 128, o_scr6)

        if "2" in PHASES:
            o_scr6 = oB_work
            o_pre = oB_work + 6200
            o_cout = oB_work + 6720
            o_tmg = oB_work + 1024

            def dt_cons2(ot, psum, ipval):
                dt_consume(oB_dt, oB_cumA, o_scr6, psum, ipval)
            inproj_tiles([OT_DT], dt_cons2)

            for g in range(G):
                def x_cons2(ot, psum, ipval, g=g):
                    j = ot - (OT_X0 + 8 * g)
                    conv_tile(ot - OT_X0, o_pre, o_cout, psum, ipval)
                    for c in range(NCH):
                        pr.trans(ar.bf16(o_tmg, 8192)[:, c * 1024 + j * 128:
                                                      c * 1024 + (j + 1) * 128],
                                 ar.bf16(o_cout, 1024)[:, c * 128:(c + 1) * 128])
                inproj_tiles(list(range(OT_X0 + 8 * g, OT_X0 + 8 * g + 8)), x_cons2)
                pr.op("sync", None, wait=[("tr", pr.cnt("tr")), ("ipc", pr.cnt("ipc"))])
                for c in range(NCH):
                    pr.dma("sync", d_xt[c, :, g * 1024:(g + 1) * 1024],
                           ar.bf16(o_tmg, 8192)[:, c * 1024:(c + 1) * 1024], sem="sp")
                pr.op("scalar", None, wait=[("sp", pr.cnt("sp"))])

                def b_cons2(ot, psum, ipval, g=g):
                    conv_tile(ot - OT_X0, o_pre, o_cout, psum, ipval)
                    pr.op("vector", lambda e, g=g: e.tensor_copy(
                        ar.bf16(o_RC, 8192)[:, g * 1024:(g + 1) * 1024],
                        ar.bf16(o_cout, 1024)), inc=[("ve", 1)])
                inproj_tiles([OT_B0 + g], b_cons2)

                def c_cons2(ot, psum, ipval, g=g):
                    conv_tile(ot - OT_X0, o_pre, o_cout, psum, ipval)
                    pr.op("vector", lambda e, g=g: e.tensor_copy(
                        ar.bf16(o_RC + 4096, 8192)[:, g * 1024:(g + 1) * 1024],
                        ar.bf16(o_cout, 1024)), inc=[("ve", 1)])
                inproj_tiles([OT_C0 + g], c_cons2)

            for gg in range(8):
                def gate_cons(ot, psum, ipval, gg=gg):
                    j = ot - 8 * gg
                    out = ar.bf16(o_cout, 1024)
                    pr.op("scalar", lambda e, out=out, psum=psum: e.copy(
                        out[:, 0:512], psum[:, 0:512]),
                        inc=[("ev", 1)], wait=[("ip", ipval), ("tr", pr.cnt("tr"))])
                    pr.op("scalar", lambda e, out=out, psum=psum: e.copy(
                        out[:, 512:1024], psum[:, 512:1024]), inc=[("ev", 1)])
                    pr.op("scalar", lambda e: e.copy(ar.f32(o_sm + 104, 1), ar.f32(o_sm + 96, 1)),
                          inc=[("ipc", 1)])
                    for c in range(NCH):
                        pr.trans(ar.bf16(o_tmg, 8192)[:, c * 1024 + j * 128:
                                                      c * 1024 + (j + 1) * 128],
                                 out[:, c * 128:(c + 1) * 128])
                inproj_tiles(list(range(8 * gg, 8 * gg + 8)), gate_cons)
                pr.op("sync", None, wait=[("tr", pr.cnt("tr")), ("ipc", pr.cnt("ipc"))])
                for c in range(NCH):
                    pr.dma("sync", d_gate[c, :, gg * 1024:(gg + 1) * 1024],
                           ar.bf16(o_tmg, 8192)[:, c * 1024:(c + 1) * 1024], sem="sp")
                pr.op("scalar", None, wait=[("sp", pr.cnt("sp"))])

        if "xnf" in dbg:
            pr.dma("sync", dbg["xnf"], ar.bf16(o_RA, 32768), sem="sp")
            pr.full_barrier()
        if "bf" in dbg:
            pr.dma("sync", dbg["bf"], ar.bf16(o_RC, 8192), sem="sp")
            pr.full_barrier()
        if "cf" in dbg:
            pr.dma("sync", dbg["cf"], ar.bf16(o_RC + 4096, 8192), sem="sp")
            pr.full_barrier()
        if "dtf" in dbg:
            pr.dma("sync", dbg["dtf"], ar.f32(oB_dt, 1024), sem="sp")
            pr.full_barrier()
        if "cumA" in dbg:
            pr.dma("sync", dbg["cumA"], ar.f32(oB_cumA, 1024), sem="sp")
            pr.full_barrier()

        if "4" in PHASES:
            oW = oB_work
            o_msk = oW
            o_zl = oW + 64
            o_apad = oW + 1088
            o_bpad = oW + 1344
            o_bev = oW + 3392
            o_btm4 = oW + 3904
            o_xs = oW + 5952
            o_acc = oW + 6976
            o_pcs = oW + 8000
            o_lg = oW + 8512
            xnf = ar.bf16(o_RA, 2 * 16384)

            for t in range(8):
                pr.wait_now("tensor", "ve", "ev", "gp")
                for k in range(NKT):
                    pr.op("tensor", lambda e, t=t, k=k: e.matmul(
                        pM[:, 0:4], xnf[:, k * 1024 + t * 128:k * 1024 + (t + 1) * 128],
                        ar.bf16(o_gatew, 128)[:, 4 * k:4 * k + 4],
                        start=(k == 0), stop=(k == NKT - 1)),
                        inc=[("mm", 1)] if k == NKT - 1 else ())
                pr.full_barrier()
                lg = ar.f32(o_lg, 8)
                pr.op("vector", lambda e, lg=lg: e.tensor_add(lg[:, 0:4], pM[:, 0:4],
                                                              ar.f32(o_ebc, 4)), inc=[("ve", 1)])
                pr.full_barrier()
                mx = ar.f32(o_sm + 24, 1)
                nd = ar.f32(o_sm + 32, 1)
                tmp = ar.f32(o_sm + 40, 1)
                pr.op("vector", lambda e, lg=lg, mx=mx: e.reduce_max(mx, lg[:, 0:4], AX.X),
                      inc=[("ve", 1)])
                pr.full_barrier()
                for e_ in range(4):
                    mcol = ar.f32(o_msk, 32)[:, t * 4 + e_:t * 4 + e_ + 1]
                    pr.op("vector", lambda e, lg=lg, mx=mx, mcol=mcol, e_=e_: e.tensor_tensor(
                        mcol, lg[:, e_:e_ + 1], mx, ALU.is_ge), inc=[("ve", 1)])
                pr.full_barrier()
                pr.op("vector", lambda e, nd=nd: e.memset(nd, 1.0), inc=[("ve", 1)])
                pr.full_barrier()
                for e_ in range(4):
                    mcol = ar.f32(o_msk, 32)[:, t * 4 + e_:t * 4 + e_ + 1]
                    if e_ > 0:
                        pr.op("vector", lambda e, mcol=mcol, nd=nd: e.tensor_mul(mcol, mcol, nd),
                              inc=[("ve", 1)])
                        pr.full_barrier()
                    if e_ < 3:
                        pr.op("vector", lambda e, mcol=mcol, tmp=tmp, nd=nd: e.tensor_mul(
                            tmp, nd, mcol), inc=[("ve", 1)])
                        pr.full_barrier()
                        pr.op("vector", lambda e, nd=nd, tmp=tmp: e.tensor_sub(nd, nd, tmp),
                              inc=[("ve", 1)])
                        pr.full_barrier()

            for half in range(2):
                for k in range(NKT):
                    apk = ar.bf16(o_apad + (k % 2) * 128, 256)
                    w = [("ip", pr.cnt("ip") - 1)] if k >= 2 else [("ve", pr.cnt("ve"))]
                    pr.op("gpsimd", lambda e, k=k, apk=apk: e.dma_start(
                        out=apk, in_=d_loraA[k, :, :]), inc=[("wl", 16)], wait=w)
                    pr.op("tensor", None, wait=[("wl", pr.cnt("wl"))])
                    for hh in range(2):
                        pr.op("tensor", lambda e, k=k, half=half, hh=hh, apk=apk: e.matmul(
                            pY1[:, hh * 512:(hh + 1) * 512],
                            apk[:, half * 128:(half + 1) * 128],
                            xnf[:, k * 1024 + hh * 512:k * 1024 + (hh + 1) * 512],
                            start=(k == 0), stop=(k == NKT - 1)),
                            inc=[("ip", 1)] if hh == 1 else ())
                pr.op("vector", lambda e, half=half: e.tensor_copy(
                    ar.bf16(o_zl + half * 512, 1024), pY1[:]),
                    wait=[("ip", pr.cnt("ip"))], inc=[("ve", 1)])
            pr.full_barrier()

            def base_cons(ot, psum, ipval):
                bev = ar.bf16(o_bev, 1024)
                pr.op("vector", lambda e, bev=bev, psum=psum: e.tensor_copy(
                    bev[:, 0:512], psum[:, 0:512]), inc=[("ve", 1)],
                    wait=[("ip", ipval), ("tr", pr.cnt("tr"))])
                pr.op("vector", lambda e, bev=bev, psum=psum: e.tensor_copy(
                    bev[:, 512:1024], psum[:, 512:1024]), inc=[("ve", 1)])
                pr.op("vector", lambda e: e.tensor_copy(ar.f32(o_sm + 104, 1),
                                                        ar.f32(o_sm + 96, 1)), inc=[("ipc", 1)])
                pr.op("scalar", None, wait=[("ve", pr.cnt("ve")), ("sp", pr.cnt("sp"))])
                for t in range(8):
                    pr.trans(ar.bf16(o_pcs, 1024)[:, t * 128:(t + 1) * 128],
                             bev[:, t * 128:(t + 1) * 128])
                pr.op("sync", None, wait=[("tr", pr.cnt("tr"))])
                for t in range(8):
                    pr.dma("sync", d_base[t, :, ot * 128:(ot + 1) * 128],
                           ar.bf16(o_pcs, 1024)[:, t * 128:(t + 1) * 128], sem="sp")
            stream_mm_tiles([(ot, d_wup[ot, :, :]) for ot in range(32)], base_cons)
            pr.full_barrier()

            pr.full_barrier()
            ebank = [pY2[:, 0:512], pY2[:, 512:1024], pT[:, 0:512], pT[:, 512:1024]]
            sp_marks, mm_marks4, ve_marks4 = [], [], []
            it = 0
            for t in range(8):
                pr.op("sync", lambda e, t=t: e.dma_start(
                    out=ar.bf16(o_btm4, 4096), in_=d_base[t, :, :]),
                    inc=[("rl", 16)], wait=[("ve", pr.cnt("ve"))])
                for os_ in range(8):
                    xs = ar.f32(o_xs + (it % 2) * 512, 512)
                    acc = ar.f32(o_acc + (it % 2) * 512, 512)
                    pr.op("sync", lambda e, xs=xs, t=t, os_=os_: e.dma_start(
                        out=xs, in_=d_xp[T_PRE + t * 128:T_PRE + (t + 1) * 128,
                                         os_ * 512:(os_ + 1) * 512]), inc=[("rl", 16)])
                    for e_ in range(4):
                        p0 = (e_ % 2) * 64
                        bpad = ar.bf16(o_bpad + (it % 2) * 1024 + e_ * 256, 512,
                                       parts=64, p0=p0)
                        w = [("ip", mm_marks4[it - 2])] if it >= 2 else []
                        pr.op("gpsimd", lambda e, e_=e_, os_=os_, bpad=bpad: e.dma_start(
                            out=bpad,
                            in_=d_loraB[e_, :, os_ * 512:(os_ + 1) * 512]),
                            inc=[("wl", 16)], wait=w if e_ == 0 else [])
                    wv = [("rl", pr.cnt("rl"))]
                    if it >= 2:
                        wv.append(("sp", sp_marks[it - 2]))
                    pr.op("vector", lambda e, acc=acc, xs=xs, os_=os_: e.tensor_add(
                        acc, ar.bf16(o_btm4, 4096)[:, os_ * 512:(os_ + 1) * 512], xs),
                        inc=[("ve", 1)], wait=wv)
                    wmm = [("wl", pr.cnt("wl"))]
                    if it >= 1:
                        wmm.append(("ve", ve_marks4[it - 1]))
                    pr.op("tensor", None, wait=wmm)
                    for e_ in range(4):
                        p0 = (e_ % 2) * 64
                        bpad = ar.bf16(o_bpad + (it % 2) * 1024 + e_ * 256, 512,
                                       parts=64, p0=p0)
                        zsl = ar.bf16(o_zl, 2048)[p0:p0 + 64,
                                                  (e_ // 2) * 1024 + t * 128:
                                                  (e_ // 2) * 1024 + (t + 1) * 128]
                        pr.op("tensor", lambda e, zsl=zsl, bpad=bpad, e_=e_: e.matmul(
                            ebank[e_], zsl, bpad, start=True, stop=True),
                            inc=[("ip", 1)] if e_ == 3 else ())
                    mm_marks4.append(pr.cnt("ip"))
                    pr.op("vector", None, wait=[("ip", mm_marks4[-1])])
                    for e_ in range(4):
                        mcol = ar.f32(o_msk, 32)[:, t * 4 + e_:t * 4 + e_ + 1]
                        pr.op("vector", lambda e, acc=acc, mcol=mcol, e_=e_:
                              e.scalar_tensor_tensor(acc, ebank[e_], mcol, acc,
                                                     ALU.mult, ALU.add), inc=[("ve", 1)])
                    ve_marks4.append(pr.cnt("ve"))
                    pr.op("sync", None, wait=[("ve", pr.cnt("ve"))])
                    pr.dma("sync", d_part[t, :, os_ * 512:(os_ + 1) * 512], acc, sem="sp")
                    sp_marks.append(pr.cnt("sp"))
                    it += 1
            pr.full_barrier()

        if "3" in PHASES:
            o_dttm = o_RA
            o_w1tm = o_RA + 1024
            o_ecatm = o_RA + 2048
            o_ncatm = o_RA + 3072
            o_xtm = o_RA + 4096
            o_y = o_RA + 8192
            o_ytmp = o_RA + 12288
            o_Tb = o_RA + 13312
            o_Mb = o_RA + 13824
            o_sg3 = o_RA + 14336
            o_rbf = o_RA + 15872
            o_gt = oB_work
            o_zc = oB_work + 4096
            o_flat = oB_work + 8192
            o_btm3 = oB_slab
            o_dc3 = oB_slab + 64
            o_w1scr = oB_slab + 1088

            dtf = ar.f32(oB_dt, 1024)
            cumA = ar.f32(oB_cumA, 1024)

            for c in range(NCH if A3LVL >= -1 else 0):
                pr.op("tensor", lambda e, c=c: e.transpose(
                    pX[:, 0:128], cumA[:, c * 128:(c + 1) * 128], ident_f), inc=[("mm", 1)])
                pr.full_barrier()
                pr.op("vector", lambda e, c=c: e.tensor_scalar(
                    ar.f32(o_ncatm, 1024)[:, c * 128:(c + 1) * 128], pX[:, 0:128],
                    -1.0, None, ALU.mult), inc=[("ve", 1)])
                pr.full_barrier()
                pr.op("scalar", lambda e, c=c: e.activation(
                    ar.f32(o_ecatm, 1024)[:, c * 128:(c + 1) * 128],
                    ar.f32(o_ncatm, 1024)[:, c * 128:(c + 1) * 128],
                    AF.Exp, scale=-1.0), inc=[("ev", 1)])
                pr.full_barrier()
                pr.op("tensor", lambda e, c=c: e.transpose(
                    pX[:, 0:128], dtf[:, c * 128:(c + 1) * 128], ident_f), inc=[("mm", 1)])
                pr.full_barrier()
                pr.op("vector", lambda e, c=c: e.tensor_copy(
                    ar.f32(o_dttm, 1024)[:, c * 128:(c + 1) * 128], pX[:, 0:128]),
                    inc=[("ve", 1)])
                pr.full_barrier()
            if A3LVL >= -1:
                w1_and_dc(oB_cumA, oB_dt, o_w1tm, o_dc3, o_w1scr)

            for c in range(NCH if A3LVL >= 0 else 0):
                pr.dma("sync", ar.bf16(o_xtm, 8192), d_xt[c, :, :], sem="rl")
                pr.dma("sync", ar.bf16(o_gt, 8192), d_gate[c, :, :], sem="rl")
                pr.full_barrier()
                if A3LVL >= 1:
                    pr.op("scalar", lambda e: e.activation(
                        ar.bf16(o_gt, 8192), ar.bf16(o_gt, 8192), AF.Silu), inc=[("ev", 1)])
                    pr.full_barrier()
                for g in range(G if A3LVL >= 2 else 0):
                    pr.op("gpsimd", lambda e, c=c, g=g: e.dma_start(
                        out=ar.f32(o_flat, 2048, parts=1),
                        in_=cumA[16 * g:16 * (g + 1), c * 128:(c + 1) * 128]),
                        inc=[("gp", 16)], wait=[("tr", pr.cnt("tr"))])
                    pr.op("scalar", None, wait=[("gp", pr.cnt("gp"))])
                    pr.trans(ar.bf16(o_btm3, 128),
                             ar.bf16(o_RC, 8192)[:, g * 1024 + c * 128:
                                                 g * 1024 + (c + 1) * 128])
                    Rg = R[:, g * 1024:(g + 1) * 1024]
                    pr.op("vector", lambda e, Rg=Rg: e.tensor_copy(
                        ar.bf16(o_rbf, 1024), Rg), inc=[("ve", 1)])
                    xsl3 = ar.bf16(o_xtm, 8192)[:, g * 1024:(g + 1) * 1024].rearrange(
                        "p (h o) -> p h o", h=16)
                    dt_ap = ar.f32(o_dttm, 1024)[:, c * 128 + 16 * g: c * 128 + 16 * (g + 1)] \
                        .rearrange("p (h o) -> p h o", o=1).broadcast_to([128, 16, 64])
                    w1_ap = ar.f32(o_w1tm, 1024)[:, c * 128 + 16 * g: c * 128 + 16 * (g + 1)] \
                        .rearrange("p (h o) -> p h o", o=1).broadcast_to([128, 16, 64])
                    eca_ap = ar.f32(o_ecatm, 1024)[:, c * 128 + 16 * g: c * 128 + 16 * (g + 1)] \
                        .rearrange("p (h o) -> p h o", o=1).broadcast_to([128, 16, 64])
                    dpl_ap = dplane_f[:, 16 * g:16 * (g + 1)].rearrange(
                        "p (h o) -> p h o", o=1).broadcast_to([128, 16, 64])
                    xdt3 = ar.bf16(o_sg3, 1024).rearrange("p (h o) -> p h o", h=16)
                    xt3 = ar.bf16(o_sg3 + 512, 1024).rearrange("p (h o) -> p h o", h=16)
                    xD3 = ar.bf16(o_sg3 + 1024, 1024).rearrange("p (h o) -> p h o", h=16)
                    pmcol = ar.f32(o_padm + 24 + c, 1)
                    pr.op("vector", lambda e, xdt3=xdt3, xsl3=xsl3, dt_ap=dt_ap:
                          e.tensor_tensor(xdt3, xsl3, dt_ap, ALU.mult), inc=[("ve", 1)])
                    pr.op("vector", lambda e, xt3=xt3, xsl3=xsl3, pmcol=pmcol, w1_ap=w1_ap:
                          e.scalar_tensor_tensor(xt3, xsl3, pmcol, w1_ap, ALU.mult, ALU.mult),
                          inc=[("ve", 1)])
                    pr.op("vector", lambda e, xD3=xD3, xsl3=xsl3, dpl_ap=dpl_ap:
                          e.tensor_tensor(xD3, xsl3, dpl_ap, ALU.mult), inc=[("ve", 1)])
                    pr.full_barrier()

                    Bf = ar.bf16(o_RC, 8192)[:, g * 1024 + c * 128: g * 1024 + (c + 1) * 128]
                    Cf = ar.bf16(o_RC + 4096, 8192)[:, g * 1024 + c * 128:
                                                    g * 1024 + (c + 1) * 128]
                    if A3LVL < 3:
                        continue
                    pr.op("tensor", lambda e, Bf=Bf, Cf=Cf: e.matmul(
                        pM[:, 0:128], Bf, Cf, start=True, stop=True), inc=[("mm", 1)])
                    pr.full_barrier()

                    for half in range(2 if A3LVL >= 4 else 0):
                        for h8 in range(8):
                            hloc = half * 8 + h8
                            fsl = ar.f32(o_flat, 2048, parts=1)[:, hloc * 128:(hloc + 1) * 128]
                            pr.op("tensor", lambda e, fsl=fsl, h8=h8: e.matmul(
                                pT[:, h8 * 128:(h8 + 1) * 128], ones_f, fsl,
                                start=True, stop=False), inc=())
                            pr.op("tensor", lambda e, h8=h8: e.matmul(
                                pT[:, h8 * 128:(h8 + 1) * 128], ident_f, maskneg_f,
                                start=False, stop=True),
                                inc=[("mm", 1)] if h8 == 7 else ())
                        pr.full_barrier()
                        for h8 in range(8):
                            hloc = half * 8 + h8
                            bcol = ar.f32(o_ncatm, 1024)[:, c * 128 + 16 * g + hloc:
                                                         c * 128 + 16 * g + hloc + 1]
                            pr.op("scalar", lambda e, h8=h8, bcol=bcol: e.activation(
                                ar.bf16(o_Tb, 1024)[:, h8 * 128:(h8 + 1) * 128],
                                pT[:, h8 * 128:(h8 + 1) * 128], AF.Exp, bias=bcol),
                                inc=[("ev", 1)])
                        pr.full_barrier()
                        gmx = pM[:, 0:128].rearrange("p (o l) -> p o l", o=1) \
                            .broadcast_to([128, 8, 128])
                        pr.op("vector", lambda e, gmx=gmx: e.tensor_tensor(
                            ar.bf16(o_Mb, 1024).rearrange("p (h l) -> p h l", h=8),
                            ar.bf16(o_Tb, 1024).rearrange("p (h l) -> p h l", h=8),
                            gmx, ALU.mult), inc=[("ve", 1)])
                        pr.full_barrier()
                        for h8 in range(8):
                            hloc = half * 8 + h8
                            pr.op("tensor", lambda e, h8=h8, hloc=hloc: e.matmul(
                                pY1[:, hloc * 64:(hloc + 1) * 64],
                                ar.bf16(o_Mb, 1024)[:, h8 * 128:(h8 + 1) * 128],
                                ar.bf16(o_sg3, 1024)[:, hloc * 64:(hloc + 1) * 64],
                                start=True, stop=False), inc=())
                            pr.op("tensor", lambda e, hloc=hloc: e.matmul(
                                pY1[:, hloc * 64:(hloc + 1) * 64], ident_b,
                                ar.bf16(o_sg3 + 1024, 1024)[:, hloc * 64:(hloc + 1) * 64],
                                start=False, stop=True),
                                inc=[("mm", 1)] if hloc == 15 else ())
                    if A3LVL < 5:
                        continue
                    for hh in range(2):
                        pr.op("tensor", lambda e, hh=hh, Cf=Cf: e.matmul(
                            pY2[:, hh * 512:(hh + 1) * 512], Cf,
                            ar.bf16(o_rbf, 1024)[:, hh * 512:(hh + 1) * 512],
                            start=True, stop=True), inc=[("mm", 1)] if hh else ())
                    pr.full_barrier()
                    pY23 = pY2[:].rearrange("p (h o) -> p h o", h=16)
                    yt3 = ar.f32(o_ytmp, 1024).rearrange("p (h o) -> p h o", h=16)
                    pr.op("vector", lambda e, yt3=yt3, pY23=pY23, eca_ap=eca_ap:
                          e.tensor_tensor(yt3, pY23, eca_ap, ALU.mult), inc=[("ve", 1)])
                    pr.full_barrier()
                    pr.op("vector", lambda e, g=g: e.tensor_add(
                        ar.bf16(o_y, 8192)[:, g * 1024:(g + 1) * 1024],
                        ar.f32(o_ytmp, 1024), pY1[:]), inc=[("ve", 1)])
                    pr.full_barrier()
                    pr.op("scalar", lambda e, g=g: e.activation(
                        ar.f32(o_ytmp, 1024), ar.bf16(o_y, 8192)[:, g * 1024:(g + 1) * 1024],
                        AF.Square, accum_out=ar.f32(o_sq + g, 1)), inc=[("ev", 1)])
                    pr.full_barrier()
                    if A3LVL < 6:
                        continue
                    for hh in range(2):
                        pr.op("tensor", lambda e, hh=hh: e.matmul(
                            pY1[:, hh * 512:(hh + 1) * 512], ar.bf16(o_btm3, 128),
                            ar.bf16(o_sg3 + 512, 1024)[:, hh * 512:(hh + 1) * 512],
                            start=True, stop=True), inc=[("mm", 1)] if hh else ())
                    dcr = ar.f32(o_dc3, 1024, parts=1)[:, c * 128:(c + 1) * 128]
                    pr.op("tensor", lambda e, dcr=dcr: e.matmul(
                        pX[:, 128:256], ones_f, dcr, start=True, stop=True), inc=[("mm", 1)])
                    pr.full_barrier()
                    dcx = pX[:, 128 + 16 * g:128 + 16 * (g + 1)].rearrange(
                        "p (h o) -> p h o", o=1).broadcast_to([128, 16, 64])
                    pr.op("vector", lambda e, Rg=Rg, dcx=dcx: e.tensor_tensor(
                        Rg.rearrange("p (h o) -> p h o", h=16),
                        Rg.rearrange("p (h o) -> p h o", h=16), dcx, ALU.mult),
                        inc=[("ve", 1)])
                    pr.full_barrier()
                    pr.op("vector", lambda e, Rg=Rg: e.tensor_add(Rg, Rg, pY1[:]),
                          inc=[("ve", 1)])
                    pr.full_barrier()

                if A3LVL == 0:
                    continue
                pr.op("vector", lambda e: e.reduce_sum(ar.f32(o_sq + 8, 1),
                                                       ar.f32(o_sq, 8), AX.X), inc=[("ve", 1)])
                pr.full_barrier()
                pr.op("scalar", lambda e: e.activation(ar.f32(o_sq + 8, 1), ar.f32(o_sq + 8, 1),
                                                       AF.Ln, bias=ar.f32(o_fc, 1), scale=1.0 / INTER),
                      inc=[("ev", 1)])
                pr.full_barrier()
                pr.op("scalar", lambda e: e.activation(ar.f32(o_sq + 8, 1), ar.f32(o_sq + 8, 1),
                                                       AF.Exp, scale=-0.5), inc=[("ev", 1)])
                pr.full_barrier()
                pr.op("vector", lambda e: e.scalar_tensor_tensor(
                    ar.bf16(o_zc, 8192), ar.bf16(o_y, 8192), ar.f32(o_sq + 8, 1),
                    ar.bf16(o_gt, 8192), ALU.mult, ALU.mult), inc=[("ve", 1)])
                pr.full_barrier()
                pr.dma("sync", d_z[c, :, :], ar.bf16(o_zc, 8192), sem="sp")
                pr.full_barrier()

        if "5" in PHASES:
            def zf_tile(kt):
                if kt < 32:
                    return ar.bf16(o_RA + kt * 512, 1024)
                if kt < 48:
                    return ar.bf16(o_RC + (kt - 32) * 512, 1024)
                return ar.bf16(o_R + (kt - 48) * 512, 1024)

            o_ztm = o_RB
            o_aev = o_RB + 4096
            o_pc5 = o_RB + 4608
            o_pp5 = o_RB + 5120
            o_ob5 = o_RB + 6144
            oW5 = o_RB + 7168

            def wslab5(i):
                return ar.bf16(oW5 + (i % 2) * 4096, 8192)

            for c in range(NCH):
                pr.op("sync", lambda e, c=c: e.dma_start(
                    out=ar.bf16(o_ztm, 8192), in_=d_z[c, :, :]),
                    inc=[("rl", 16)], wait=[("tr", pr.cnt("tr"))])
                pr.op("scalar", None, wait=[("rl", pr.cnt("rl"))])
                for kt in range(64):
                    pr.trans(zf_tile(kt)[:, c * 128:(c + 1) * 128],
                             ar.bf16(o_ztm, 8192)[:, kt * 128:(kt + 1) * 128])
            pr.full_barrier()

            nwx = ar.f32(o_normw, 64).rearrange("p (k o) -> p k o", o=1) \
                .broadcast_to([128, 64, 128])
            mm_marks5, c_marks5 = [], []
            for ot in range(32):
                w = [("ip", mm_marks5[ot - 2])] if ot >= 2 else []
                pr.op("gpsimd", lambda e, ot=ot: e.dma_start(
                    out=wslab5(ot), in_=d_wout[ot, :, :]), inc=[("wl", 16)], wait=w)
                pr.op("vector", lambda e, ot=ot, nwx=nwx: e.tensor_tensor(
                    wslab5(ot).rearrange("p (k o) -> p k o", k=64),
                    wslab5(ot).rearrange("p (k o) -> p k o", k=64), nwx, ALU.mult),
                    inc=[("ve", 1)], wait=[("wl", pr.cnt("wl"))])
                psum = pT if ot % 2 == 0 else pY1
                wmm = [("ve", pr.cnt("ve"))]
                if ot >= 2:
                    wmm.append(("ipc", c_marks5[ot - 2]))
                pr.op("tensor", None, wait=wmm)
                for kt in range(64):
                    for hh in range(2):
                        pr.op("tensor", lambda e, ot=ot, kt=kt, hh=hh, psum=psum: e.matmul(
                            psum[:, hh * 512:(hh + 1) * 512],
                            wslab5(ot)[:, kt * 128:(kt + 1) * 128],
                            zf_tile(kt)[:, hh * 512:(hh + 1) * 512],
                            start=(kt == 0), stop=(kt == 63)),
                            inc=[("ip", 1)] if (kt == 63 and hh == 1) else ())
                mm_marks5.append(pr.cnt("ip"))
                aev = ar.bf16(o_aev, 1024)
                pr.op("vector", lambda e, aev=aev, psum=psum: e.tensor_copy(
                    aev[:, 0:512], psum[:, 0:512]),
                    inc=[("ve", 1)], wait=[("ip", mm_marks5[-1]), ("fin", pr.cnt("fin"))])
                pr.op("vector", lambda e, aev=aev, psum=psum: e.tensor_copy(
                    aev[:, 512:1024], psum[:, 512:1024]), inc=[("ve", 1)])
                pr.op("scalar", None, wait=[("ve", pr.cnt("ve"))])
                for t in range(8):
                    pr.trans(ar.bf16(o_pc5, 1024)[:, t * 128:(t + 1) * 128],
                             aev[:, t * 128:(t + 1) * 128])
                pr.op("sync", None, wait=[("ve", pr.cnt("ve"))])
                for t in range(8):
                    pr.dma("sync", ar.f32(o_pp5, 1024)[:, t * 128:(t + 1) * 128],
                           d_part[t, :, ot * 128:(ot + 1) * 128], sem="rl")
                pr.op("vector", lambda e: e.tensor_add(
                    ar.f32(o_ob5, 1024), ar.bf16(o_pc5, 1024), ar.f32(o_pp5, 1024)),
                    inc=[("ve", 1)],
                    wait=[("tr", pr.cnt("tr")), ("rl", pr.cnt("rl"))])
                pr.op("vector", lambda e: e.tensor_copy(ar.f32(o_sm + 96, 1),
                                                        ar.f32(o_sm + 96, 1)),
                      inc=[("ipc", 1)])
                c_marks5.append(pr.cnt("ipc"))
                pr.op("sync", None, wait=[("ve", pr.cnt("ve"))])
                for t in range(8):
                    pr.dma("sync", d_out[t * 128:(t + 1) * 128, ot * 128:(ot + 1) * 128],
                           ar.f32(o_ob5, 1024)[:, t * 128:(t + 1) * 128], sem="fin")
            pr.full_barrier()

        pr.barrier("fin")

        block = ctx.enter_context(nc.Block())
        pr.replay(block)
    return nc


def _host_inputs(inputs):
    f = np.float32
    x = np.ascontiguousarray(inputs["L0_attn_hidden_states"], dtype=f)
    W_in = np.ascontiguousarray(inputs["L0_attn_in_proj_w"], dtype=f)
    conv_w = np.ascontiguousarray(inputs["L0_attn_conv_w"], dtype=f)
    conv_b = np.ascontiguousarray(inputs["L0_attn_conv_b"], dtype=f)
    dt_bias = np.ascontiguousarray(inputs["L0_attn_dt_bias"], dtype=f)
    A_log = np.ascontiguousarray(inputs["L0_attn_A_log"], dtype=f)
    D = np.ascontiguousarray(inputs["L0_attn_D"], dtype=f)
    norm_w = np.ascontiguousarray(inputs["L0_attn_norm_w"], dtype=f)
    W_out = np.ascontiguousarray(inputs["L0_attn_out_proj_w"], dtype=f)
    W_up = np.ascontiguousarray(inputs["L0_ffn_W_up"], dtype=f)
    gate_w = np.ascontiguousarray(inputs["L0_ffn_gate_w"], dtype=f)
    ebias = np.ascontiguousarray(inputs["L0_ffn_expert_bias"], dtype=f)
    lora_A = np.ascontiguousarray(inputs["L0_ffn_lora_A"], dtype=f)
    lora_B = np.ascontiguousarray(inputs["L0_ffn_lora_B"], dtype=f)

    win = np.ascontiguousarray(
        W_in.reshape(N_OT, 128, NKT, 128).transpose(0, 3, 2, 1).reshape(N_OT, 128, DIM))
    wout = np.ascontiguousarray(
        W_out.reshape(32, 128, 64, 128).transpose(0, 3, 2, 1).reshape(32, 128, INTER))
    wup = np.ascontiguousarray(
        W_up.reshape(32, 128, 32, 128).transpose(0, 3, 2, 1).reshape(32, 128, DIM))
    gatew = np.ascontiguousarray(
        gate_w.T.reshape(32, 128, 4).transpose(1, 0, 2).reshape(128, 128))
    loraA = np.zeros((DIM, 256), f)
    for e in range(4):
        loraA[:, e * 64:e * 64 + 8] = lora_A[e].T
    loraA = np.ascontiguousarray(loraA.reshape(32, 128, 256))
    loraB = np.zeros((4, 64, DIM), f)
    for e in range(4):
        loraB[e, 0:8, :] = lora_B[e].T
    convw = np.ascontiguousarray(
        conv_w[:, 0, :].reshape(80, 128, KCONV).transpose(1, 0, 2).reshape(128, 80 * KCONV))
    convb = np.ascontiguousarray(conv_b.reshape(80, 128).T)
    normw = np.ascontiguousarray(norm_w.reshape(64, 128).T)

    ones = np.ones((1, 128), f)
    ident = np.eye(128, dtype=f)
    maskneg = np.where(np.arange(128)[None, :] < np.arange(128)[:, None],
                       f(MASKVAL), f(0.0))

    shared = dict(win=win, wout=wout, wup=wup, gatew=gatew,
                  ebias=ebias.reshape(1, 4), loraA=loraA, loraB=loraB,
                  convw=convw, convb=convb, dtb=dt_bias.reshape(128, 1),
                  alog=A_log.reshape(128, 1), drow=D.reshape(1, 128),
                  normw=normw, onesrow=ones, ident=ident, maskneg=maskneg,
                  fconst=np.full((128, 4), EPS, dtype=f))

    in_maps = []
    for k in range(8):
        b, q = divmod(k, 4)
        start = q * T_OWN
        xp = np.zeros((T_ALL, DIM), f)
        xp[T_PRE - start:T_PRE] = x[b, :start]
        xp[T_PRE:] = x[b, start:start + T_OWN]
        pm = np.zeros((128, 32), f)
        pm[:, (T_PRE - start) // 128:] = 1.0
        m = dict(shared)
        m["xp"] = xp
        m["padmask"] = pm
        in_maps.append(m)
    return in_maps


def kernel(**inputs):
    in_maps = _host_inputs(inputs)
    nc = build_program()
    res = run_bass_kernel_spmd(nc, in_maps, core_ids=list(range(8)))
    out = np.empty((2, 4096, DIM), np.float32)
    for k in range(8):
        b, q = divmod(k, 4)
        out[b, q * T_OWN:(q + 1) * T_OWN] = res.results[k]["out"]
    return out
